# version 37
# baseline (speedup 1.0000x reference)
import contextlib
import math
import numpy as np
import ml_dtypes

import concourse.bass as bass
import concourse.mybir as mybir
from concourse.bass_utils import run_bass_kernel_spmd
from concourse.aot_env import allow_ici_all_to_all

F32 = mybir.dt.float32
BF16 = mybir.dt.bfloat16
FP16 = mybir.dt.float16
I32 = mybir.dt.int32
A = mybir.AluOpType
AF = mybir.ActivationFunctionType

T = 8192
D = 4096
H = 11008
DH = 128
NC = 8
TOWN = 1024
NTT = 2
TT = 512
DC = D // 128
HC = H // 128
BLK = 64
EPS = 1e-5
SCALE = 1.0 / math.sqrt(DH)

UPS = 11

NF4 = np.array([-1.0, -0.6961928009986877, -0.5250730514526367, -0.39491748809814453,
                -0.28444138169288635, -0.18477343022823334, -0.09105003625154495, 0.0,
                0.07958029955625534, 0.16093020141124725, 0.24611230194568634,
                0.33791524171829224, 0.44070982933044434, 0.5626170039176941,
                0.7229568362236023, 1.0], dtype=np.float64)


ENGINES = ("sync", "scalar", "vector", "tensor", "gpsimd")
ESEM = {"sync": "syn", "scalar": "act", "vector": "dve",
        "tensor": "pe", "gpsimd": "gpc"}

SYNC_STREAMS = ["cst", "xsA", "xsB", "wsA", "wsB", "rpA", "rpB", "xhl",
                "qdA", "qdB", "cxl", "cdA", "cdB"]
GP_STREAMS = ["lra", "stg", "wgA", "wgB", "stA", "stB",
              "qsA", "qsB", "xps", "ctp", "bdc"]
SEM_NAMES = (["pe", "act", "dve", "gpc", "syn",
              "ccu", "ccd", "ccx", "ccc"] + SYNC_STREAMS + GP_STREAMS)


class P:

    def __init__(self, nc, sems, cur):
        self.nc = nc
        self.sems = sems
        self.cur = cur
        self.cnt = {s: 0 for s in sems}
        self.waited = {}
        self.eng = {n: getattr(nc, n) for n in ENGINES}

    def val(self, sem):
        return self.cnt[sem]

    def emit(self, engine, build=None, waits=(), inc=None, by=1):
        for s, v in waits:
            if v <= 0:
                continue
            key = (engine, s)
            if self.waited.get(key, 0) < v:
                self.waited[key] = v
                if engine == self.cur:
                    self.eng[engine].wait_ge(self.sems[s], v)
        r = None
        if build is not None and engine == self.cur:
            r = build(self.eng[engine])
        if inc is not None:
            self.cnt[inc] += by
            if build is not None and engine == self.cur:
                r.then_inc(self.sems[inc], by)
        return r

    def barrier(self):
        self.emit("sync", lambda e: e.nop(),
                  waits=[(s, self.val(s)) for s in SYNC_STREAMS])
        self.emit("gpsimd", lambda e: e.nop(),
                  waits=[(s, self.val(s)) for s in GP_STREAMS])
        marks = {}
        for e in ENGINES:
            self.emit(e, lambda en: en.nop(), inc=ESEM[e])
            marks[e] = self.val(ESEM[e])
        for e in ENGINES:
            ws = [(ESEM[o], marks[o]) for o in ENGINES if o != e]
            self.emit(e, lambda en: en.nop(), waits=ws)



def build_program():
    import os
    rd = os.environ.get("K_RACEDET", "0") == "1"
    nc = bass.Bass("TRN2", target_bir_lowering=False, detect_race_conditions=rd)
    io = {}

    def ein(name, shape, dt):
        io[name] = nc.dram_tensor(name, shape, dt, kind="ExternalInput").ap()

    def eout(name, shape, dt):
        io[name] = nc.dram_tensor(name, shape, dt, kind="ExternalOutput").ap()

    def ind(name, shape, dt):
        io[name] = nc.dram_tensor(name, shape, dt).ap()

    ein("resid_in", [D, TOWN], F32)
    ein("wq_s", [4, 128, DC, 128], F32)
    ein("wk_s", [1, 128, DC, 128], F32)
    ein("wv_s", [1, 128, DC, 128], F32)
    ein("wo_s", [DC, 128, DC, 128], F32)
    ein("upc", [UPS * 128, D], I32)
    ein("upam", [128, UPS * BLK], F32)
    ein("dnc", [512, H], I32)
    ein("dnam", [128, 4 * (H // BLK)], F32)
    ein("bup", [128, HC], F32)
    ein("bdn", [128, DC], F32)
    ein("aupT", [16, D], F32)
    ein("adnT", [16, H], F32)
    ein("bu_sl", [16, UPS * 128], F32)
    ein("bd_sl", [16, 512], F32)
    ein("ropec", [128, T], F32)
    ein("ropes", [128, T], F32)
    ein("masks", [128, 4 * TT], BF16)
    ein("ident", [128, 128], F32)

    eout("out_d", [D, TOWN], F32)

    ind("x_mid", [D, TOWN], F32)
    ind("x_attn", [D, TOWN], F32)
    ind("wup_part", [UPS, 128, DC, 128], BF16)
    io["wup_all"] = nc.dram_tensor("wup_all", [NC * UPS, 128, DC, 128], BF16,
                                   addr_space="Shared").ap()
    ind("wdn_part", [2, 4, 128, 43, 128], BF16)
    io["wdn_all"] = nc.dram_tensor("wdn_all", [2, 4, NC, 128, 43, 128], BF16,
                                   addr_space="Shared").ap()
    ind("xh_part", [NTT, 128, DC, TT], BF16)
    io["xh_all"] = nc.dram_tensor("xh_all", [NC * NTT, 128, DC, TT], BF16,
                                   addr_space="Shared").ap()
    ind("ctx_part", [NC, 4, 128, TOWN], BF16)
    ind("ctx_own", [NC, 4, 128, TOWN], BF16)
    ind("q_dram", [4, 128, T], BF16)
    ind("dbg1", [128, DC * TT], BF16)
    ind("dbg2", [128, 43 * TT], BF16)

    stack = contextlib.ExitStack()
    sb = {}

    def sbuf(name, shape, dt):
        sb[name] = stack.enter_context(nc.sbuf_tensor(name, shape, dt))

    def psum(name, shape, dt):
        sb[name] = stack.enter_context(nc.psum_tensor(name, shape, dt))

    with stack:
        sbuf("xh", [128, DC * TT], BF16)
        sbuf("x2", [128, 43 * TT], BF16)
        sbuf("karena", [128, T], BF16)
        sbuf("varena", [128, T], BF16)
        sbuf("wsb", [128, 11008], BF16)
        sbuf("dq_w", [128, 2 * 1024], I32)
        sbuf("dq_stg", [128, 1024], BF16)
        sbuf("dq_amu", [128, 704], F32)
        sbuf("dq_amd", [128, 688], F32)
        sbuf("dq_bu", [16, 128], BF16)
        sbuf("dq_at", [16, 1024], BF16)
        sbuf("rc", [128, 2 * TT], F32)
        sbuf("rs", [128, 2 * TT], F32)
        sbuf("mk", [128, 4 * TT], BF16)
        sbuf("idn", [128, 128], F32)
        sbuf("bup_sb", [128, HC], F32)
        sbuf("bdn_sb", [128, DC], F32)
        sbuf("bdnh", [128, DC], F32)
        sbuf("ones_c", [128, 1], BF16)
        sbuf("ones_rf", [1, 128], F32)
        sbuf("ones_r16", [1, 128], FP16)
        sbuf("r16", [1, 512], FP16)
        sbuf("xs", [128, 2 * TT], F32)
        sbuf("xbf", [128, TT], BF16)
        sbuf("xsq", [128, TT], BF16)
        sbuf("rows", [1, 4 * TT], F32)
        sbuf("mb", [128, TT], F32)
        sbuf("rb_", [128, TT], F32)
        sbuf("va", [128, TT], F32)
        sbuf("vb", [128, TT], F32)
        sbuf("eb", [128, 2 * 2048], BF16)
        sbuf("em", [128, 2048], BF16)
        sb["dq_ct"] = sb["dq_w"].ap()
        sb["dq_stt"] = bass.AP(sb["karena"].ap().tensor.bitcast(F32), 0,
                               [[4096, 128], [1, 1024]])
        sb["dq_cf"] = bass.AP(sb["eb"].ap().tensor.bitcast(FP16), 0,
                              [[4096, 128], [1, 2048]])
        sb["dq_acc"] = bass.AP(sb["em"].ap().tensor.bitcast(FP16), 0,
                               [[2048, 128], [1, 1024]])
        sb["dq_trm"] = bass.AP(sb["em"].ap().tensor.bitcast(FP16), 1024,
                               [[2048, 128], [1, 1024]])
        sbuf("lb", [128, TT], F32)
        sbuf("qd", [128, 2 * TT], BF16)
        sbuf("cstg", [128, TOWN], BF16)
        sbuf("ostg", [128, 2 * TT], F32)
        psum("psA", [128, 2048], F32)
        psum("ps4", [128, 512], F32)
        psum("ps5", [128, 512], F32)
        psum("ps6", [128, 512], F32)
        psum("ps7", [128, 512], F32)
        for i in range(4):
            sb[f"ps{i}"] = sb["psA"].ap()[:, i * 512:(i + 1) * 512]

        semstack = contextlib.ExitStack()
        sems = {}
        with semstack, nc.Block() as block:
            for name in SEM_NAMES:
                sems[name] = semstack.enter_context(nc.semaphore("s_" + name))

            def build_all(cur):
                p = P(nc, sems, cur)
                emit_program(p, io, sb)

            @block.sync
            def _(eng):
                build_all("sync")

            @block.scalar
            def _(eng):
                build_all("scalar")

            @block.vector
            def _(eng):
                build_all("vector")

            @block.tensor
            def _(eng):
                build_all("tensor")

            @block.gpsimd
            def _(eng):
                build_all("gpsimd")

    return nc


def phase_ffn1(p, io, sb):
    DQ_COMMON["side"] = None
    st_up = dq_state()
    st_dn = dq_state()

    def dn_half(hh):
        for dcl in range(4):
            dq_run(p, io, sb, st_dn, "dn",
                   [(dcl, hh, pc) for pc in range(6)])
            dq_dn_gather(p, io, sb, hh, dcl, p.val("stg"))

    def hook(tt, half, stage, i):
        if tt != 0:
            return
        if (half, stage) == (0, "up"):
            dq_up_slabs(p, io, sb, st_up, 0, 2)
        elif stage == "up_hc":
            hc = half * 43 + i
            if hc % 8 == 0 and hc // 8 in st_up.get("stg_r", {}):
                dq_up_gather(p, io, sb, st_up, hc // 8)
            if half == 0 and i in (8, 16, 24, 32):
                dq_up_slabs(p, io, sb, st_up, i // 8 + 1, i // 8 + 2)
        elif (half, stage) == (0, "down"):
            dn_half(0)
        elif (half, stage) == (0, "down_dc") and i in (5, 11, 17, 23, 29):
            s = 6 + (5, 11, 17, 23, 29).index(i)
            dq_up_slabs(p, io, sb, st_up, s, s + 1)
            dq_up_gather(p, io, sb, st_up, s)
        elif (half, stage) == (1, "down"):
            dn_half(1)

    phase_ffn(p, io, sb, src="resid_in", dst="x_mid", hook=hook)


def emit_program(p, io, sb):
    import os
    np_ = int(os.environ.get("K_PHASES", "99"))
    phases = [
        lambda: phase_consts(p, io, sb),
        lambda: phase_ffn1(p, io, sb),
        lambda: phase_ln2qkv(p, io, sb),
        lambda: phase_attn(p, io, sb),
        lambda: phase_wo(p, io, sb),
        lambda: phase_ffn(p, io, sb, src="x_attn", dst="out_d"),
    ]
    np_ += 1
    for i, ph in enumerate(phases):
        if i >= np_:
            break
        ph()
        p.barrier()
    dump = os.environ.get("K_DUMP", "")
    if dump:
        if ":" in dump:
            name, off = dump.split(":")
            off = int(off)
            nel = D * TOWN * 2
            tsz = 1
            for s_ in io[name].tensor.shape:
                tsz *= s_
            bpe = {BF16: 2, F32: 4, I32: 4}[io[name].tensor.dtype]
            nel = min(nel, tsz * bpe // 2)
            flat = bass.AP(io[name].tensor.bitcast(BF16), off, [[1, nel]])
            dstv = bass.AP(io["out_d"].tensor.bitcast(BF16), 0, [[1, nel]])
            p.emit("gpsimd", lambda e, s_=flat, d_=dstv: e.dma_start(out=d_, in_=s_),
                   inc="stA", by=16)
        else:
            p.emit("gpsimd", lambda e: e.dma_start(out=io["out_d"][:, :],
                                                   in_=io[dump][:, :]),
                   inc="stA", by=16)
    p.emit("gpsimd", lambda e: e.nop(),
           waits=[(s, p.val(s)) for s in GP_STREAMS])


def pe_fence(p, sb):
    p.emit("tensor",
           lambda e: e.matmul(sb["ps5"][0:1, 0:1], sb["ones_c"][:, :],
                              sb["ones_c"][:, :], start=True, stop=True),
           inc="pe")



def phase_consts(p, io, sb):
    p.emit("sync", lambda e: e.dma_start(out=sb["idn"][:, :], in_=io["ident"][:, :]),
           inc="cst", by=16)
    p.emit("sync", lambda e: e.dma_start(out=sb["mk"][:, :], in_=io["masks"][:, :]),
           inc="cst", by=16)
    p.emit("sync", lambda e: e.dma_start(out=sb["bup_sb"][:, :], in_=io["bup"][:, :]),
           inc="cst", by=16)
    p.emit("sync", lambda e: e.dma_start(out=sb["bdn_sb"][:, :], in_=io["bdn"][:, :]),
           inc="cst", by=16)
    p.emit("sync", lambda e: e.dma_start(out=sb["dq_amu"][:, :], in_=io["upam"][:, 0:704]),
           inc="cst", by=16)
    p.emit("sync", lambda e: e.dma_start(out=sb["dq_amd"][:, :], in_=io["dnam"][:, :]),
           inc="cst", by=16)
    p.emit("vector", lambda e: e.memset(sb["ones_c"][:, :], 1.0), inc="dve")
    p.emit("vector", lambda e: e.memset(sb["ones_rf"][:, :], 1.0), inc="dve")
    p.emit("vector", lambda e: e.memset(sb["ones_r16"][:, :], 1.0), inc="dve")
    p.emit("vector",
           lambda e: e.tensor_scalar(sb["bdnh"][:, :], sb["bdn_sb"][:, :], 0.5,
                                     None, A.mult),
           waits=[("cst", p.val("cst"))], inc="dve")



def dq_state():
    return dict(act_mark={}, dve_mark={}, stg_free={}, pe_mark={},
                lora_ld={}, i=0)


DQ_COMMON = {"side": None}


def dq_run(p, io, sb, st, side, tiles):
    nf4 = NF4
    ct, cf = sb["dq_ct"], sb["dq_cf"]
    codes = io["upc"] if side == "up" else io["dnc"]
    act_mark = st["act_mark"]
    dve_mark = st["dve_mark"]
    stg_free = st["stg_free"]
    pe_mark = st["pe_mark"]
    lora_ld = st["lora_ld"]
    amtab = sb["dq_amu"] if side == "up" else sb["dq_amd"]
    if DQ_COMMON["side"] != side:
        DQ_COMMON["side"] = side
        ent = {"act": p.val("act"), "pe": p.val("pe"),
               "dve": p.val("dve"), "stg": p.val("stg")}
    else:
        ent = {"act": 0, "pe": 0, "dve": 0, "stg": 0}
    kk = 0
    first_tile = True

    for tile in tiles:
        i = st["i"]
        st["i"] += 1
        if side == "up":
            s, q = tile
            F = 1024
            src = codes[s * 128:(s + 1) * 128, q * 1024:q * 1024 + F]
        else:
            dcl, hh, pc = tile
            F = 1024 if pc < 5 else 384
            col0 = hh * 5504 + pc * 1024
            src = codes[dcl * 128:(dcl + 1) * 128, col0:col0 + F]
        nj = F // 128
        if True:
            for hi, h2 in enumerate(range(0, F, 1024)):
                fh = min(1024, F - h2)
                sidx = i + hi
                cbuf = sidx % 2
                cdsem = "cdA" if cbuf == 0 else "cdB"
                p.emit("sync",
                       lambda e, o=ct[:, cbuf * 1024:cbuf * 1024 + fh],
                       s_=src[:, h2:h2 + fh]: e.dma_start(out=o, in_=s_),
                       waits=[("dve", dve_mark.get(("cast", sidx - 2), 0))],
                       inc=cdsem, by=16)
                ldm = (cdsem, p.val(cdsem))
                p.emit("vector",
                       lambda e, o=cf[:, cbuf * 1024:cbuf * 1024 + fh],
                       s_=ct[:, cbuf * 1024:cbuf * 1024 + fh]:
                       e.tensor_copy(o, s_),
                       waits=[ldm], inc="dve")
                dve_mark[("cast", sidx)] = p.val("dve")
                cfv = cf[:, cbuf * 1024:cbuf * 1024 + fh]
                accv = sb["dq_acc"][:, 0:fh]
                trmv = sb["dq_trm"][:, 0:fh]
                p.emit("vector",
                       lambda e, o=accv, s_=cfv: e.tensor_scalar(
                           o, s_, 0.0, float(nf4[0]), A.is_equal, A.mult),
                       inc="dve")
                for v in range(1, 16):
                    p.emit("vector",
                           lambda e, o=trmv, s_=cfv, vv=v: e.tensor_scalar(
                               o, s_, float(vv), float(nf4[vv]), A.is_equal, A.mult),
                           inc="dve")
                    p.emit("vector",
                           lambda e, o=accv, t_=trmv: e.tensor_tensor(o, o, t_, A.add),
                           inc="dve")
                dve_mark[("acc", sidx)] = p.val("dve")
                nb = fh // BLK
                if side == "up":
                    off = s * BLK + (q * 1024 + h2) // BLK
                else:
                    off = dcl * (H // BLK) + (col0 + h2) // BLK
                amv = amtab[:, off:off + nb]
                am_b = amv.rearrange("p (b one) -> p b one", one=1).broadcast_to(
                    [128, nb, BLK])
                accb = accv.rearrange("p (b r) -> p b r", r=BLK)
                sttv = sb["dq_stt"][:, h2:h2 + fh].rearrange("p (b r) -> p b r", r=BLK)
                p.emit("vector",
                       lambda e, o=sttv, s_=accb, m=am_b: e.scalar_tensor_tensor(
                           o, s_, 0.0, m, A.add, A.mult),
                       waits=[("pe", max(pe_mark.get(("tr", i - 1), 0),
                                         ent["pe"] if kk < 2 else 0))],
                       inc="dve")
                dve_mark[("stt", sidx)] = p.val("dve")
                kk += 1
            stt_done = p.val("dve")

            if side == "up":
                at_src = io["aupT"][:, q * 1024:q * 1024 + F]
                b_src = io["bu_sl"][:, s * 128:(s + 1) * 128]
            else:
                at_src = io["adnT"][:, col0:col0 + F]
                b_src = io["bd_sl"][:, dcl * 128:(dcl + 1) * 128]
            p.emit("gpsimd",
                   lambda e, s_=at_src, f=F: e.dma_start(out=sb["dq_at"][:, 0:f], in_=s_),
                   waits=[("pe", max(pe_mark.get(("tr", i - 1), 0),
                                     ent["pe"] if first_tile else 0))],
                   inc="lra", by=16)
            p.emit("gpsimd",
                   lambda e, s_=b_src: e.dma_start(out=sb["dq_bu"][:, :], in_=s_),
                   inc="lra", by=16)
            lora_ld[i] = p.val("lra")

            wv = [("dve", stt_done), ("lra", lora_ld[i])]
            for j in range(nj):
                bank = sb["ps6"] if j < 4 else sb["ps7"]
                col = (j % 4) * 128
                bfirst = (j % 4 == 0)
                blast = (j == nj - 1) or (j % 4 == 3)
                p.emit("tensor",
                       lambda e, b_=bank, c=col, j_=j, f=bfirst: e.matmul(
                           b_[:, c:c + 128], sb["dq_stt"][:, j_ * 128:(j_ + 1) * 128],
                           sb["idn"][:, :], start=f, stop=False, is_transpose=True),
                       waits=wv if j == 0 else (), inc="pe")
                p.emit("tensor",
                       lambda e, b_=bank, c=col, j_=j, l_=blast: e.matmul(
                           b_[:, c:c + 128], sb["dq_at"][:, j_ * 128:(j_ + 1) * 128],
                           sb["dq_bu"][:, :], start=False, stop=l_),
                       inc="pe")
            pe_mark[("tr", i)] = p.val("pe")

            sgb = 0
            p.emit("vector",
                   lambda e, o=sb["dq_stg"][:, sgb:sgb + min(nj, 4) * 128]:
                   e.tensor_copy(o, sb["ps6"][:, 0:min(nj, 4) * 128]),
                   waits=[("pe", pe_mark[("tr", i)]),
                          ("stg", max(stg_free.get(i - 1, 0),
                                      ent["stg"] if first_tile else 0))],
                   inc="dve")
            if nj > 4:
                p.emit("vector",
                       lambda e, o=sb["dq_stg"][:, sgb + 512:sgb + 512 + (nj - 4) * 128]:
                       e.tensor_copy(o, sb["ps7"][:, 0:(nj - 4) * 128]),
                       inc="dve")
            act_mark[("evac", i)] = p.val("dve")

            if side == "up":
                dst = io["wup_part"][s, :, 8 * q:8 * q + nj, :].rearrange(
                    "p c l -> p (c l)")
            else:
                dst = io["wdn_part"][hh, dcl, :, 8 * pc:8 * pc + nj, :].rearrange(
                    "p c l -> p (c l)")
            p.emit("gpsimd",
                   lambda e, d=dst, o=sb["dq_stg"][:, sgb:sgb + nj * 128]:
                   e.dma_start(out=d, in_=o),
                   waits=[("dve", act_mark[("evac", i)])], inc="stg", by=16)
            stg_free[i] = p.val("stg")
            first_tile = False


def dq_up_slabs(p, io, sb, st, s0, s1):
    for s in range(s0, s1):
        dq_run(p, io, sb, st, "up", [(s, q) for q in range(4)])
        st.setdefault("stg_r", {})[s] = p.val("stg")


def dq_up_gather(p, io, sb, st, r):
    if r in st.setdefault("g_done", set()):
        return
    st["g_done"].add(r)
    p.emit("gpsimd",
           lambda e, s_=r: e.collective_compute(
               "AllGather", A.bypass, replica_groups=[list(range(NC))],
               ins=[io["wup_part"][s_].rearrange("p c l -> p (c l)")],
               outs=[io["wup_all"][8 * s_:8 * s_ + 8]
                     .rearrange("s p c l -> (s p) (c l)")]),
           waits=[("stg", st["stg_r"][r])], inc="ccu")


def dq_dn_gather(p, io, sb, hh, dcl, stg_mark):
    p.emit("gpsimd",
           lambda e, h=hh, d=dcl: e.collective_compute(
               "AllGather", A.bypass, replica_groups=[list(range(NC))],
               ins=[io["wdn_part"][h, d].rearrange("p c l -> p (c l)")],
               outs=[io["wdn_all"][h, d].rearrange("s p c l -> (s p) (c l)")]),
           waits=[("stg", stg_mark)], inc="ccd")



def snap(p):
    return {s: p.val(s) for s in ("pe", "act", "dve", "stA", "stB")}


def emit_ln(p, io, sb, src_ap, tt, lnpart=9, carry=None):
    if carry is None:
        carry = snap(p)
    ps_sum, ps_sq = sb["ps4"], sb["ps5"]
    xs = sb["xs"]
    ld = {}
    cast_m = {}
    pe_m = {}
    for dc in range(DC):
        buf = dc % 2
        xsem = "xsA" if buf == 0 else "xsB"
        p.emit("sync",
               lambda e, b=buf, d=dc: e.dma_start(
                   out=xs[:, b * TT:(b + 1) * TT],
                   in_=src_ap[d * 128:(d + 1) * 128, tt * TT:(tt + 1) * TT]),
               waits=[("act", cast_m.get(dc - 2, 0)),
                      ("dve", carry["dve"] if dc < 2 else 0)],
               inc=xsem, by=16)
        ld[dc] = (xsem, p.val(xsem))
        p.emit("scalar",
               lambda e, b=buf: e.copy(sb["xbf"][:, :], xs[:, b * TT:(b + 1) * TT]),
               waits=[ld[dc], ("pe", pe_m.get(dc - 1, carry["pe"] if dc < 1 else 0))],
               inc="act")
        p.emit("scalar",
               lambda e, b=buf: e.activation(
                   out=sb["xsq"][:, :], in_=xs[:, b * TT:(b + 1) * TT],
                   func=AF.Square),
               inc="act")
        cast_m[dc] = p.val("act")
        p.emit("tensor",
               lambda e, d=dc: e.matmul(ps_sum[0:1, :], sb["ones_c"][:, :],
                                        sb["xbf"][:, :], start=(d == 0),
                                        stop=(d == DC - 1)),
               waits=[("act", cast_m[dc])], inc="pe")
        p.emit("tensor",
               lambda e, d=dc: e.matmul(ps_sq[0:1, :], sb["ones_c"][:, :],
                                        sb["xsq"][:, :], start=(d == 0),
                                        stop=(d == DC - 1)),
               inc="pe")
        pe_m[dc] = p.val("pe")
    stats_pe = p.val("pe")
    stats_act = p.val("act")
    if lnpart <= 1:
        return p.val("dve")
    rows = sb["rows"]
    m_row = rows[0:1, 0 * TT:1 * TT]
    s_row = rows[0:1, 1 * TT:2 * TT]
    v_row = rows[0:1, 2 * TT:3 * TT]
    r_row = rows[0:1, 3 * TT:4 * TT]
    p.emit("scalar",
           lambda e: e.activation(out=m_row, in_=ps_sum[0:1, :], func=AF.Copy,
                                  bias=0.0, scale=1.0 / D),
           waits=[("pe", stats_pe)], inc="act")
    p.emit("scalar",
           lambda e: e.activation(out=s_row, in_=ps_sq[0:1, :], func=AF.Copy,
                                  bias=0.0, scale=1.0 / D),
           inc="act")
    p.emit("vector", lambda e: e.tensor_tensor(v_row, m_row, m_row, A.mult),
           waits=[("act", p.val("act"))], inc="dve")
    p.emit("vector", lambda e: e.tensor_tensor(v_row, s_row, v_row, A.subtract),
           inc="dve")
    p.emit("vector",
           lambda e: e.tensor_scalar(v_row, v_row, EPS, None, A.add),
           inc="dve")
    p.emit("scalar",
           lambda e: e.activation(out=s_row, in_=v_row, func=AF.Sqrt),
           waits=[("dve", p.val("dve"))], inc="act")
    d_row = rows[0:1, 2 * TT:3 * TT]
    p.emit("scalar",
           lambda e, d=d_row: e.copy(d, s_row),
           inc="act")
    p.emit("vector", lambda e: e.reciprocal(r_row, s_row),
           waits=[("act", p.val("act"))], inc="dve")
    if lnpart <= 2:
        return p.val("dve")
    p.emit("tensor",
           lambda e: e.matmul(sb["ps6"][:, :], sb["ones_rf"][:, :], m_row,
                              start=True, stop=True),
           waits=[("dve", p.val("dve")), ("act", carry["act"])], inc="pe")
    p.emit("tensor",
           lambda e: e.matmul(sb["ps7"][:, :], sb["ones_rf"][:, :], r_row,
                              start=True, stop=True),
           inc="pe")
    bc_pe = p.val("pe")
    pe_fence(p, sb)
    if lnpart <= 3:
        return p.val("dve")
    p.emit("vector", lambda e: e.tensor_copy(sb["mb"][:, :], sb["ps6"][:, :]),
           waits=[("pe", p.val("pe")), ("dve", p.val("dve"))], inc="dve")
    p.emit("vector", lambda e: e.tensor_copy(sb["rb_"][:, :], sb["ps7"][:, :]),
           inc="dve")
    bc_dve = p.val("dve")
    if lnpart <= 4:
        return p.val("dve")
    ap_m = {}
    for dc in range(DC):
        buf = dc % 2
        xsem = "xsA" if buf == 0 else "xsB"
        w = [("dve", ap_m.get(dc - 2, 0))]
        if dc < 2:
            w.append(("act", stats_act))
        p.emit("sync",
               lambda e, b=buf, d=dc: e.dma_start(
                   out=xs[:, b * TT:(b + 1) * TT],
                   in_=src_ap[d * 128:(d + 1) * 128, tt * TT:(tt + 1) * TT]),
               waits=w, inc=xsem, by=16)
        mark = (xsem, p.val(xsem))
        p.emit("vector",
               lambda e, b=buf: e.tensor_tensor(sb["va"][:, :],
                                                xs[:, b * TT:(b + 1) * TT],
                                                sb["mb"][:, :], A.subtract),
               waits=[mark, ("dve", bc_dve)], inc="dve")
        p.emit("vector",
               lambda e, d=dc: e.tensor_tensor(
                   sb["xh"][:, d * TT:(d + 1) * TT], sb["va"][:, :],
                   sb["rb_"][:, :], A.mult),
               inc="dve")
        ap_m[dc] = p.val("dve")
    return p.val("dve")



def phase_ffn(p, io, sb, src, dst, hook=None):
    final = (dst == "out_d")
    src_ap = io[src]
    inter = io["x_mid"]
    wsb = sb["wsb"]
    for tt in range(NTT):
        ln_done = emit_ln(p, io, sb, src_ap, tt)
        for half in range(2):
            if hook:
                hook(tt, half, "up", 0)
            cu = snap(p)
            relu_m = {}
            up_pe = {}
            slab_ld = {}
            for hcl in range(43):
                hc = half * 43 + hcl
                if hook:
                    hook(tt, half, "up_hc", hcl)
                buf = hcl % 2
                wsem = "wsA" if buf == 0 else "wsB"
                p.emit("sync",
                       lambda e, b=buf, s=hc: e.dma_start(
                           out=wsb[:, b * D:(b + 1) * D],
                           in_=io["wup_all"][s].rearrange("p c l -> p (c l)")),
                       waits=[("pe", up_pe.get(hcl - 2, cu["pe"])),
                              ("ccu", hc // 8 + 1)],
                       inc=wsem, by=16)
                slab_ld[hcl] = (wsem, p.val(wsem))
                bank = sb[f"ps{hcl % 2}"]
                for dc in range(DC):
                    w = []
                    if dc == 0:
                        w = [slab_ld[hcl],
                             ("act", relu_m.get(hcl - 2, cu["act"])),
                             ("dve", ln_done)]
                    p.emit("tensor",
                           lambda e, b=bank, bf=buf, d=dc: e.matmul(
                               b[:, :],
                               wsb[:, bf * D + d * 128:bf * D + (d + 1) * 128],
                               sb["xh"][:, d * TT:(d + 1) * TT],
                               start=(d == 0), stop=(d == DC - 1)),
                           waits=w, inc="pe")
                up_pe[hcl] = p.val("pe")
                p.emit("scalar",
                       lambda e, b=bank, h=hc, hl=hcl: e.activation(
                           out=sb["x2"][:, hl * TT:(hl + 1) * TT], in_=b[:, :],
                           func=AF.Relu, bias=sb["bup_sb"][:, h:h + 1], scale=1.0),
                       waits=[("pe", up_pe[hcl])], inc="act")
                relu_m[hcl] = p.val("act")
            up_done_act = p.val("act")
            up_pe_last = p.val("pe")
            if hook:
                hook(tt, half, "down", 0)
            cd = snap(p)
            dn_ld = {}
            dn_pe = {}
            dn_ev = {}
            dn_st = {}
            rd = src_ap if half == 0 else inter
            wr = inter if half == 0 else io[dst]
            for di, dc in enumerate(dcl + 4 * k
                                    for dcl in range(4) for k in range(8)):
                if hook:
                    hook(tt, half, "down_dc", di)
                buf = di % 2
                wsem = "wsA" if buf == 0 else "wsB"
                xsem = "xsA" if buf == 0 else "xsB"
                ccd_w = ("ccd", half * 4 + dc % 4 + 1)
                wv = [("pe", dn_pe.get(di - 2, 0)), ccd_w]
                if di < 2:
                    wv = [("pe", up_pe_last), ccd_w]
                p.emit("sync",
                       lambda e, b=buf, d=dc, h=half: e.dma_start(
                           out=wsb[:, b * 5504:(b + 1) * 5504],
                           in_=io["wdn_all"][h, d % 4, d // 4]
                           .rearrange("p c l -> p (c l)")),
                       waits=wv, inc=wsem, by=16)
                dn_ld[di] = (wsem, p.val(wsem))
                wv = [("dve", dn_ev.get(di - 2, max(ln_done, cd["dve"])
                                        if di < 2 else 0))]
                if half == 1 and di < 2:
                    wv += [("stA", cd["stA"]), ("stB", cd["stB"])]
                p.emit("sync",
                       lambda e, b=buf, d=dc, r=rd: e.dma_start(
                           out=sb["xs"][:, b * TT:(b + 1) * TT],
                           in_=r[d * 128:(d + 1) * 128, tt * TT:(tt + 1) * TT]),
                       waits=wv, inc=xsem, by=16)
                xs_ld = (xsem, p.val(xsem))
                bank = sb[f"ps{2 + di % 2}"]
                for hcl in range(43):
                    w = []
                    if hcl == 0:
                        w = [dn_ld[di], ("act", up_done_act),
                             ("dve", dn_ev.get(di - 2, cd["dve"]))]
                    last = (hcl == 42)
                    p.emit("tensor",
                           lambda e, b=bank, bf=buf, hl=hcl, l_=last: e.matmul(
                               b[:, :],
                               wsb[:, bf * 5504 + hl * 128:bf * 5504 + (hl + 1) * 128],
                               sb["x2"][:, hl * TT:(hl + 1) * TT],
                               start=(hl == 0), stop=l_),
                           waits=w, inc="pe")
                dn_pe[di] = p.val("pe")
                ob = di % 2
                wprev = dn_st.get(di - 2,
                                  ("stA" if ob == 0 else "stB",
                                   cd["stA" if ob == 0 else "stB"]))
                w = [("pe", dn_pe[di]), xs_ld, wprev]
                if half == 1:
                    p.emit("vector",
                           lambda e, b=bank, d=dc: e.tensor_scalar(
                               sb["va"][:, :], b[:, :], 0.5,
                               sb["bdnh"][:, d:d + 1], A.mult, A.add),
                           waits=w, inc="dve")
                    p.emit("vector",
                           lambda e, bf=buf, o=ob: e.tensor_tensor(
                               sb["ostg"][:, o * TT:(o + 1) * TT], sb["va"][:, :],
                               sb["xs"][:, bf * TT:(bf + 1) * TT], A.add),
                           inc="dve")
                else:
                    p.emit("vector",
                           lambda e, b=bank, bf=buf, o=ob: e.scalar_tensor_tensor(
                               sb["ostg"][:, o * TT:(o + 1) * TT], b[:, :], 0.5,
                               sb["xs"][:, bf * TT:(bf + 1) * TT], A.mult, A.add),
                           waits=w, inc="dve")
                dn_ev[di] = p.val("dve")
                ssem = "stA" if ob == 0 else "stB"
                p.emit("gpsimd",
                       lambda e, d=dc, o=ob, w_=wr: e.dma_start(
                           out=w_[d * 128:(d + 1) * 128, tt * TT:(tt + 1) * TT],
                           in_=sb["ostg"][:, o * TT:(o + 1) * TT]),
                       waits=[("dve", dn_ev[di])], inc=ssem, by=16)
                dn_st[di] = (ssem, p.val(ssem))



def phase_ln2qkv(p, io, sb):
    wsb = sb["wsb"]
    for oc in range(6):
        if oc < 4:
            wsrc, dst = io["wq_s"][oc], sb["x2"][:, oc * D:(oc + 1) * D]
        elif oc == 4:
            wsrc, dst = io["wk_s"][0], wsb[:, 0:D]
        else:
            wsrc, dst = io["wv_s"][0], wsb[:, D:2 * D]
        p.emit("gpsimd",
               lambda e, s_=wsrc, d_=dst: e.dma_start(
                   out=d_, in_=s_.rearrange("p c l -> p (c l)")),
               inc="wgA", by=16)
    slab_ld = ("wgA", p.val("wgA"))

    for tt in range(NTT):
        if tt == 1:
            p.emit("vector", lambda e: e.nop(),
                   waits=[("xps", p.val("xps"))])
        ln_done = emit_ln(p, io, sb, io["x_mid"], tt)
        for dc in range(DC):
            p.emit("gpsimd",
                   lambda e, d=dc, t_=tt: e.dma_start(
                       out=io["xh_part"][t_, :, d, :],
                       in_=sb["xh"][:, d * TT:(d + 1) * TT]),
                   waits=[("dve", ln_done)], inc="xps", by=16)
        p.emit("gpsimd",
               lambda e, t_=tt: e.collective_compute(
                   "AllGather", A.bypass, replica_groups=[list(range(NC))],
                   ins=[io["xh_part"][t_].rearrange("p c t -> p (c t)")],
                   outs=[io["xh_all"][t_ * NC:(t_ + 1) * NC]
                         .rearrange("a p c t -> (a p) (c t)")]),
               waits=[("xps", p.val("xps"))], inc="ccx")
    xps_all = p.val("xps")

    def slabv(oc, d):
        if oc < 4:
            return sb["x2"][:, oc * D + d * 128:oc * D + (d + 1) * 128]
        return wsb[:, (oc - 4) * D + d * 128:(oc - 4) * D + (d + 1) * 128]

    wld = {}
    ev = {}
    for tg, (ttl, rank) in enumerate(
            (t, r) for t in range(NTT) for r in range(NC)):
        gt = rank * NTT + ttl
        w = [("pe", ev.get(("pe_last", tg - 1), 0)), ("ccx", ttl + 1)]
        if tg < 2:
            w.append(("xps", xps_all))
        p.emit("sync",
               lambda e, t_=ttl, r=rank: e.dma_start(
                   out=sb["xh"][:, :],
                   in_=io["xh_all"][t_ * NC + r].rearrange("p c t -> p (c t)")),
               waits=w, inc="xhl", by=16)
        xh_ld = ("xhl", p.val("xhl"))
        rbuf = tg % 2
        rpsem = "rpA" if rbuf == 0 else "rpB"
        p.emit("sync",
               lambda e, b=rbuf, t_=gt: e.dma_start(
                   out=sb["rc"][:, b * TT:(b + 1) * TT],
                   in_=io["ropec"][:, t_ * TT:(t_ + 1) * TT]),
               waits=[("dve", ev.get(("rope_dve", tg - 2), 0))], inc=rpsem, by=16)
        p.emit("sync",
               lambda e, b=rbuf, t_=gt: e.dma_start(
                   out=sb["rs"][:, b * TT:(b + 1) * TT],
                   in_=io["ropes"][:, t_ * TT:(t_ + 1) * TT]),
               inc=rpsem, by=16)
        rope_ld = (rpsem, p.val(rpsem))
        for oc in range(6):
            gi = tg * 6 + oc
            bank = sb[f"ps{oc % 2}"]
            for dc in range(DC):
                w = []
                if dc == 0:
                    w = [slab_ld, xh_ld,
                         ("dve", ev.get(("cons", gi - 2), 0)),
                         ("act", ev.get(("vcons", gi - 2), 0))]
                p.emit("tensor",
                       lambda e, o=oc, d=dc, b=bank: e.matmul(
                           b[:, :], slabv(o, d),
                           sb["xh"][:, d * TT:(d + 1) * TT],
                           start=(d == 0), stop=(d == DC - 1)),
                       waits=w, inc="pe")
            wld[gi] = p.val("pe")
            mmdone = p.val("pe")
            if oc < 5:
                rcv = sb["rc"][:, rbuf * TT:(rbuf + 1) * TT]
                rsv = sb["rs"][:, rbuf * TT:(rbuf + 1) * TT]
                p.emit("vector",
                       lambda e, b=bank, r=rcv: e.tensor_tensor(
                           sb["va"][:, :], b[:, :], r, A.mult),
                       waits=[("pe", mmdone), rope_ld], inc="dve")
                p.emit("vector",
                       lambda e, b=bank, r=rsv: e.tensor_tensor(
                           sb["vb"][0:64, :], b[64:128, :], r[0:64, :], A.mult),
                       inc="dve")
                p.emit("vector",
                       lambda e, b=bank, r=rsv: e.tensor_tensor(
                           sb["vb"][64:128, :], b[0:64, :], r[64:128, :], A.mult),
                       inc="dve")
                ev[("cons", gi)] = p.val("dve")
                if oc < 4:
                    qb2 = (tg * 4 + oc) % 2
                    dstv = sb["qd"][:, qb2 * TT:(qb2 + 1) * TT]
                    qssem = "qsA" if qb2 == 0 else "qsB"
                    p.emit("vector",
                           lambda e, o=dstv: e.tensor_tensor(
                               o, sb["va"][:, :], sb["vb"][:, :], A.add),
                           waits=[ev.get(("qst", tg * 4 + oc - 2), (qssem, 0))],
                           inc="dve")
                    qdve = p.val("dve")
                    p.emit("gpsimd",
                           lambda e, o=dstv, h=oc, t_=gt: e.dma_start(
                               out=io["q_dram"][h, :, t_ * TT:(t_ + 1) * TT], in_=o),
                           waits=[("dve", qdve)], inc=qssem, by=16)
                    ev[("qst", tg * 4 + oc)] = (qssem, p.val(qssem))
                else:
                    p.emit("vector",
                           lambda e, t_=gt: e.tensor_tensor(
                               sb["karena"][:, t_ * TT:(t_ + 1) * TT],
                               sb["va"][:, :], sb["vb"][:, :], A.add),
                           inc="dve")
                ev[("rope_dve", tg)] = p.val("dve")
                ev[("cons", gi)] = p.val("dve")
            else:
                p.emit("vector",
                       lambda e, b=bank: e.tensor_copy(sb["va"][:, :], b[:, :]),
                       waits=[("pe", mmdone),
                              ("pe", ev.get(("vtr", tg - 1), 0))], inc="dve")
                ev[("cons", gi)] = p.val("dve")
                vdve = p.val("dve")
                for j in range(4):
                    b2 = sb["ps6"] if j < 2 else sb["ps7"]
                    col = (j % 2) * 128
                    p.emit("tensor",
                           lambda e, bb=b2, c=col, j_=j: e.matmul(
                               bb[:, c:c + 128],
                               sb["va"][:, j_ * 128:(j_ + 1) * 128],
                               sb["idn"][:, :], start=(j_ % 2 == 0),
                               stop=(j_ % 2 == 1), is_transpose=True),
                           waits=[("dve", vdve),
                                  ("act", ev.get(("vcons", gi - 1), 0))]
                           if j == 0 else (),
                           inc="pe")
                ev[("vtr", tg)] = p.val("pe")
                p.emit("scalar",
                       lambda e, t_=gt: e.copy(
                           sb["varena"][:, (t_ * 4) * 128:(t_ * 4 + 2) * 128],
                           sb["ps6"][:, 0:256]),
                       waits=[("pe", ev[("vtr", tg)])], inc="act")
                p.emit("scalar",
                       lambda e, t_=gt: e.copy(
                           sb["varena"][:, (t_ * 4 + 2) * 128:(t_ * 4 + 4) * 128],
                           sb["ps7"][:, 0:256]),
                       inc="act")
                ev[("vcons", gi)] = p.val("act")
        ev[("pe_last", tg)] = p.val("pe")



def phase_attn(p, io, sb):
    eb, em = sb["eb"], sb["em"]
    o_bank, l_bank = sb["ps4"], sb["ps5"]
    qld = {}
    prev_fin_dve = 0
    prev_lb_dve = 0
    prev_pair_exp = [0, 0]
    for qh in range(4):
        for qb in range(16):
            gi = qh * 16 + qb
            qbuf = gi % 2
            qdsem = "qdA" if qbuf == 0 else "qdB"
            p.emit("sync",
                   lambda e, h=qh, b=qb, q_=qbuf: e.dma_start(
                       out=sb["qd"][:, q_ * TT:(q_ + 1) * TT],
                       in_=io["q_dram"][h, :, b * TT:(b + 1) * TT]),
                   waits=[("pe", qld.get(("pe", gi - 2), 0))], inc=qdsem, by=16)
            qld[gi] = (qdsem, p.val(qdsem))
            nhb = 2 * (qb + 1)
            exp_m = {}
            lpv_m = {}
            cons = {}

            def lpv(hb, last):
                c_, cw = cons[hb]
                first = (hb == 0)
                if first:
                    cw = cw + [("dve", prev_fin_dve)]
                for kc in range(2):
                    p.emit("tensor",
                           lambda e, c=c_, k=kc, f=first: e.matmul(
                               l_bank[0:1, 0:TT], sb["ones_c"][:, :],
                               c[:, k * TT:(k + 1) * TT],
                               start=(f and k == 0), stop=(last and k == 1)),
                           waits=cw if kc == 0 else (), inc="pe")
                for kc in range(2):
                    gc = 2 * hb + kc
                    p.emit("tensor",
                           lambda e, c=c_, k=kc, g=gc, f=first: e.matmul(
                               o_bank[:, :],
                               sb["varena"][:, g * 128:(g + 1) * 128],
                               c[:, k * TT:(k + 1) * TT],
                               start=(f and k == 0), stop=(last and k == 1)),
                           inc="pe")
                lpv_m[hb] = p.val("pe")

            for hb in range(nhb):
                pair = sb["psA"].ap()[:, (hb % 2) * 1024:(hb % 2 + 1) * 1024]
                for kc in range(2):
                    w = []
                    if kc == 0:
                        w = [("act", exp_m.get(hb - 2, prev_pair_exp[hb % 2]))]
                        if hb == 0:
                            w.append(qld[gi])
                    p.emit("tensor",
                           lambda e, pr=pair, k=(2 * hb + kc) * 128, c=kc,
                           q_=qbuf: e.matmul(
                               pr[:, c * 512:(c + 1) * 512],
                               sb["karena"][:, k:k + 128],
                               sb["qd"][:, q_ * TT:(q_ + 1) * TT],
                               start=True, stop=True),
                           waits=w, inc="pe")
                s_done = p.val("pe")
                if hb > 0:
                    lpv(hb - 1, False)
                ebv = eb[:, (hb % 2) * 1024:(hb % 2 + 1) * 1024]
                p.emit("scalar",
                       lambda e, o=ebv, pr=pair: e.activation(
                           out=o, in_=pr, func=AF.Exp, scale=SCALE),
                       waits=[("pe", s_done), ("pe", lpv_m.get(hb - 2, 0))],
                       inc="act")
                exp_m[hb] = p.val("act")
                if hb // 2 == qb:
                    emv = em[:, (hb % 2) * 1024:(hb % 2 + 1) * 1024]
                    p.emit("vector",
                           lambda e, o=emv, s_=ebv, m=sb["mk"][
                               :, (hb % 2) * 1024:(hb % 2 + 1) * 1024]:
                           e.tensor_tensor(o, s_, m, A.mult),
                           waits=[("act", exp_m[hb]),
                                  ("pe", lpv_m.get(hb - 1, 0))], inc="dve")
                    cons[hb] = (emv, [("dve", p.val("dve"))])
                else:
                    cons[hb] = (ebv, [("act", exp_m[hb])])
            lpv(nhb - 1, True)
            prev_pair_exp = [exp_m[nhb - 2], exp_m[nhb - 1]]
            pv_done = p.val("pe")
            r_row = sb["rows"][0:1, 3 * TT:4 * TT]
            p.emit("vector", lambda e, r=r_row: e.reciprocal(r, l_bank[0:1, 0:TT]),
                   waits=[("pe", pv_done)], inc="dve")
            p.emit("scalar",
                   lambda e, r=r_row: e.activation(
                       out=sb["r16"][0:1, :], in_=r, func=AF.Copy,
                       bias=0.0, scale=1.0),
                   waits=[("dve", p.val("dve"))], inc="act")
            p.emit("scalar",
                   lambda e, r=r_row: e.copy(sb["rows"][0:1, 0:TT], r),
                   inc="act")
            p.emit("tensor",
                   lambda e: e.matmul(sb["ps6"][:, :], sb["ones_r16"][:, :],
                                      sb["r16"][0:1, :], start=True, stop=True),
                   waits=[("act", p.val("act")), ("dve", prev_lb_dve)], inc="pe")
            p.emit("vector",
                   lambda e: e.tensor_copy(sb["lb"][:, :], sb["ps6"][:, :]),
                   waits=[("pe", p.val("pe"))], inc="dve")
            prev_lb_dve = p.val("dve")
            p.emit("vector",
                   lambda e, b=qb: e.tensor_tensor(
                       sb["cstg"][:, (b % 2) * TT:(b % 2 + 1) * TT],
                       o_bank[:, :], sb["lb"][:, :], A.mult),
                   waits=[("ctp", qld.get(("st", gi - 1), 0))], inc="dve")
            prev_fin_dve = p.val("dve")
            qld[("pe", gi)] = p.val("pe")
            p.emit("gpsimd",
                   lambda e, h=qh, b=qb: e.dma_start(
                       out=io["ctx_part"][b // 2, h, :, (b % 2) * TT:(b % 2 + 1) * TT],
                       in_=sb["cstg"][:, (b % 2) * TT:(b % 2 + 1) * TT]),
                   waits=[("dve", prev_fin_dve)], inc="ctp", by=16)
            qld[("st", gi)] = p.val("ctp")

    def do_a2a(e):
        with allow_ici_all_to_all():
            return e.collective_compute(
                "AllToAll", A.bypass, replica_groups=[list(range(NC))],
                ins=[io["ctx_part"].rearrange("a h p t -> (a h p) t")],
                outs=[io["ctx_own"].rearrange("a h p t -> (a h p) t")])

    p.emit("gpsimd", do_a2a, waits=[("ctp", p.val("ctp"))], inc="ccc")
    for e in ENGINES:
        p.emit(e, lambda en: en.nop(), waits=[("ccc", 1)])



def phase_wo(p, io, sb):
    wsb = sb["wsb"]

    def ctxv(c):
        if c < 16:
            return sb["xh"][:, c * 1024:(c + 1) * 1024]
        return sb["x2"][:, (c - 16) * 1024:(c - 15) * 1024]

    for c in range(DC):
        for t2 in range(2):
            p.emit("sync",
                   lambda e, c_=c, t_=t2: e.dma_start(
                       out=ctxv(c_)[:, t_ * TT:(t_ + 1) * TT],
                       in_=io["ctx_own"][c_ // 4, c_ % 4, :,
                                         t_ * TT:(t_ + 1) * TT]),
                   inc="cxl", by=16)
    ctx_ld = ("cxl", p.val("cxl"))
    wld = {}
    ev = {}
    st = {}
    for oc in range(DC):
        buf = oc % 2
        wgsem = "wgA" if buf == 0 else "wgB"
        p.emit("gpsimd",
               lambda e, b=buf, o=oc: e.dma_start(
                   out=wsb[:, b * D:(b + 1) * D],
                   in_=io["wo_s"][o].rearrange("p c l -> p (c l)")),
               waits=[("pe", wld.get(oc - 2, 0))], inc=wgsem, by=16)
        wl = (wgsem, p.val(wgsem))
        xsem = "xsA" if buf == 0 else "xsB"
        xbuf = sb["xs"][:, 0:2 * TT] if buf == 0 else sb["rs"][:, 0:2 * TT]
        p.emit("sync",
               lambda e, o=oc, xb=xbuf: e.dma_start(
                   out=xb,
                   in_=io["x_mid"][o * 128:(o + 1) * 128, :]),
               waits=[("dve", ev.get(oc - 2, 0))], inc=xsem, by=16)
        xl = (xsem, p.val(xsem))
        for t2 in range(2):
            bank = sb[f"ps{buf * 2 + t2}"]
            for dc in range(DC):
                w = []
                if dc == 0 and t2 == 0:
                    w = [wl, ctx_ld, ("dve", ev.get(oc - 2, 0))]
                p.emit("tensor",
                       lambda e, b=bank, bf=buf, d=dc, t_=t2: e.matmul(
                           b[:, :],
                           wsb[:, bf * D + d * 128:bf * D + (d + 1) * 128],
                           ctxv(d)[:, t_ * TT:(t_ + 1) * TT],
                           start=(d == 0), stop=(d == DC - 1)),
                       waits=w, inc="pe")
        wld[oc] = p.val("pe")
        obuf = sb["ostg"][:, 0:2 * TT] if buf == 0 else sb["rc"][:, 0:2 * TT]
        w = [("pe", wld[oc]), xl, st.get(oc - 2, ("stA", 0))]
        for t2 in range(2):
            bank = sb[f"ps{buf * 2 + t2}"]
            p.emit("vector",
                   lambda e, b=bank, t_=t2, ob=obuf, xb=xbuf:
                   e.scalar_tensor_tensor(
                       ob[:, t_ * TT:(t_ + 1) * TT], b[:, :], 0.0,
                       xb[:, t_ * TT:(t_ + 1) * TT], A.add, A.add),
                   waits=w if t2 == 0 else (), inc="dve")
        ev[oc] = p.val("dve")
        ssem = "stA" if buf == 0 else "stB"
        p.emit("gpsimd",
               lambda e, o=oc, ob=obuf: e.dma_start(
                   out=io["x_attn"][o * 128:(o + 1) * 128, :],
                   in_=ob),
               waits=[("dve", ev[oc])], inc=ssem, by=16)
        st[oc] = (ssem, p.val(ssem))



def _rope_tables():
    import jax
    import jax.numpy as jnp
    cpu = jax.local_devices(backend="cpu")[0]
    with jax.default_device(cpu):
        return _rope_tables_impl(jnp)


def _rope_tables_impl(jnp):
    ROPE = dict(rope_theta=500000.0, factor=32.0, hi_freq_factor=4.0,
                lo_freq_factor=1.0, original_context_length=8192)
    head_dim = DH
    pos = jnp.arange(T)
    idx = jnp.arange(0, head_dim, 2, dtype=jnp.float32) / head_dim
    freq = 1.0 / (2.0 * math.pi) * ROPE['rope_theta'] ** (-idx)
    factor, lo, hi = (ROPE['factor'], ROPE['lo_freq_factor'], ROPE['hi_freq_factor'])
    L0 = ROPE['original_context_length']
    freq_low, freq_high = (lo / L0, hi / L0)
    freq_scaled = jnp.where(freq < freq_low, freq / factor, freq)
    smooth = jnp.clip((L0 * freq - lo) / (hi - lo), 0.0, 1.0)
    freq_smooth = (1.0 - smooth) * (freq / factor) + smooth * freq
    is_mid = (freq >= freq_low) & (freq <= freq_high)
    freq = jnp.where(is_mid, freq_smooth, freq_scaled)
    phase = 2.0 * math.pi * pos.astype(jnp.float32)[:, None] * freq[None, :]
    emb = jnp.concatenate([phase, phase], axis=-1)
    cos = np.asarray(jnp.cos(emb), dtype=np.float32)
    sin = np.asarray(jnp.sin(emb), dtype=np.float32)
    cos_d = np.ascontiguousarray(cos.T)
    sinrot = np.ascontiguousarray(sin.T)
    sinrot[0:64, :] *= -1.0
    return cos_d, sinrot


def _slab(w):
    Dd, M = w.shape
    return np.ascontiguousarray(
        w.reshape(Dd // 128, 128, M // 128, 128).transpose(2, 1, 0, 3))


def kernel(L0_attn_resid, L0_attn_Wq, L0_attn_Wk, L0_attn_Wv, L0_attn_Wo,
           L0_ffn_w_up_codes, L0_ffn_w_up_absmax, L0_ffn_b_up,
           L0_ffn_w_up_lora_a, L0_ffn_w_up_lora_b, L0_ffn_w_down_codes,
           L0_ffn_w_down_absmax, L0_ffn_b_down, L0_ffn_w_down_lora_a,
           L0_ffn_w_down_lora_b):
    resid = np.ascontiguousarray(np.asarray(L0_attn_resid, dtype=np.float32)[0].T)
    wq = np.asarray(L0_attn_Wq, dtype=np.float32)
    wk = np.asarray(L0_attn_Wk, dtype=np.float32)
    wv = np.asarray(L0_attn_Wv, dtype=np.float32)
    wo = np.asarray(L0_attn_Wo, dtype=np.float32)
    upc = np.asarray(L0_ffn_w_up_codes, dtype=np.int32).reshape(H, D)
    upam = np.asarray(L0_ffn_w_up_absmax, dtype=np.float32).reshape(H, D // BLK)
    dnc = np.asarray(L0_ffn_w_down_codes, dtype=np.int32).reshape(D, H)
    dnam = np.asarray(L0_ffn_w_down_absmax, dtype=np.float32).reshape(D, H // BLK)
    bup = np.asarray(L0_ffn_b_up, dtype=np.float32)
    bdn = np.asarray(L0_ffn_b_down, dtype=np.float32)
    aup = np.asarray(L0_ffn_w_up_lora_a, dtype=np.float32)
    bu = np.asarray(L0_ffn_w_up_lora_b, dtype=np.float32)
    adn = np.asarray(L0_ffn_w_down_lora_a, dtype=np.float32)
    bd = np.asarray(L0_ffn_w_down_lora_b, dtype=np.float32)

    cos_d, sinrot = _rope_tables()
    masks = np.zeros((128, 4 * TT), dtype=ml_dtypes.bfloat16)
    for kc in range(4):
        kg = kc * 128 + np.arange(128)[:, None]
        qg = np.arange(TT)[None, :]
        masks[:, kc * TT:(kc + 1) * TT] = (kg <= qg).astype(ml_dtypes.bfloat16)
    ident = np.eye(128, dtype=np.float32)

    wo_slab = _slab(wo)
    bup_h = np.ascontiguousarray(bup.reshape(HC, 128).T)
    aupT = np.ascontiguousarray(aup.T)
    adnT = np.ascontiguousarray(adn.T)

    in_maps = []
    for c in range(NC):
        t0 = c * TOWN
        slabs = list(range(c, HC, NC))
        upc_sl = np.zeros((UPS * 128, D), dtype=np.int32)
        upam_sl = np.zeros((UPS, 128, BLK), dtype=np.float32)
        bu_sl = np.zeros((16, UPS * 128), dtype=np.float32)
        for j, g in enumerate(slabs):
            upc_sl[j * 128:(j + 1) * 128] = upc[g * 128:(g + 1) * 128]
            upam_sl[j] = upam[g * 128:(g + 1) * 128]
            bu_sl[:, j * 128:(j + 1) * 128] = bu[:, g * 128:(g + 1) * 128]
        upam_sl = np.ascontiguousarray(
            upam_sl.transpose(1, 0, 2).reshape(128, UPS * BLK))
        d0 = c * 512
        dnc_sl = np.ascontiguousarray(dnc[d0:d0 + 512])
        dnam_sl = np.ascontiguousarray(
            dnam[d0:d0 + 512].reshape(4, 128, H // BLK).transpose(1, 0, 2)
            .reshape(128, 4 * (H // BLK)))
        bd_sl = np.ascontiguousarray(bd[:, d0:d0 + 512])

        in_maps.append({
            "resid_in": np.ascontiguousarray(resid[:, t0:t0 + TOWN]),
            "wq_s": _slab(wq[:, c * 512:(c + 1) * 512]),
            "wk_s": _slab(wk[:, c * 128:(c + 1) * 128]),
            "wv_s": _slab(wv[:, c * 128:(c + 1) * 128]),
            "wo_s": wo_slab,
            "upc": upc_sl,
            "upam": upam_sl,
            "dnc": dnc_sl,
            "dnam": dnam_sl,
            "bup": bup_h,
            "bdn": np.ascontiguousarray(bdn.reshape(DC, 128).T),
            "aupT": aupT,
            "adnT": adnT,
            "bu_sl": bu_sl,
            "bd_sl": bd_sl,
            "ropec": cos_d,
            "ropes": sinrot,
            "masks": masks,
            "ident": ident,
        })

    nc = build_program()
    res = run_bass_kernel_spmd(nc, in_maps, core_ids=list(range(NC)))
    out = np.empty((1, T, D), dtype=np.float32)
    for c in range(NC):
        out[0, c * TOWN:(c + 1) * TOWN, :] = res.results[c]["out_d"].T
    return out


# version 39
# speedup vs baseline: 1.0122x; 1.0122x over previous
import contextlib
import math
import numpy as np
import ml_dtypes

import concourse.bass as bass
import concourse.mybir as mybir
from concourse.bass_utils import run_bass_kernel_spmd
from concourse.aot_env import allow_ici_all_to_all

F32 = mybir.dt.float32
BF16 = mybir.dt.bfloat16
FP16 = mybir.dt.float16
I32 = mybir.dt.int32
A = mybir.AluOpType
AF = mybir.ActivationFunctionType

T = 8192
D = 4096
H = 11008
DH = 128
NC = 8
TOWN = 1024
NTT = 2
TT = 512
DC = D // 128
HC = H // 128
BLK = 64
EPS = 1e-5
SCALE = 1.0 / math.sqrt(DH)

UPS = 11

NF4 = np.array([-1.0, -0.6961928009986877, -0.5250730514526367, -0.39491748809814453,
                -0.28444138169288635, -0.18477343022823334, -0.09105003625154495, 0.0,
                0.07958029955625534, 0.16093020141124725, 0.24611230194568634,
                0.33791524171829224, 0.44070982933044434, 0.5626170039176941,
                0.7229568362236023, 1.0], dtype=np.float64)


ENGINES = ("sync", "scalar", "vector", "tensor", "gpsimd")
ESEM = {"sync": "syn", "scalar": "act", "vector": "dve",
        "tensor": "pe", "gpsimd": "gpc"}

SYNC_STREAMS = ["cst", "xsA", "xsB", "wsA", "wsB", "rpA", "rpB", "xhl",
                "qdA", "qdB", "cxl", "cdA", "cdB"]
GP_STREAMS = ["lra", "stg", "wgA", "wgB", "stA", "stB",
              "qsA", "qsB", "xps", "ctp", "bdc"]
SEM_NAMES = (["pe", "act", "dve", "gpc", "syn",
              "ccu", "ccd", "ccx", "ccc"] + SYNC_STREAMS + GP_STREAMS)


class P:

    def __init__(self, nc, sems, cur):
        self.nc = nc
        self.sems = sems
        self.cur = cur
        self.cnt = {s: 0 for s in sems}
        self.waited = {}
        self.eng = {n: getattr(nc, n) for n in ENGINES}

    def val(self, sem):
        return self.cnt[sem]

    def emit(self, engine, build=None, waits=(), inc=None, by=1):
        for s, v in waits:
            if v <= 0:
                continue
            key = (engine, s)
            if self.waited.get(key, 0) < v:
                self.waited[key] = v
                if engine == self.cur:
                    self.eng[engine].wait_ge(self.sems[s], v)
        r = None
        if build is not None and engine == self.cur:
            r = build(self.eng[engine])
        if inc is not None:
            self.cnt[inc] += by
            if build is not None and engine == self.cur:
                r.then_inc(self.sems[inc], by)
        return r

    def barrier(self):
        self.emit("sync", lambda e: e.nop(),
                  waits=[(s, self.val(s)) for s in SYNC_STREAMS])
        self.emit("gpsimd", lambda e: e.nop(),
                  waits=[(s, self.val(s)) for s in GP_STREAMS])
        marks = {}
        for e in ENGINES:
            self.emit(e, lambda en: en.nop(), inc=ESEM[e])
            marks[e] = self.val(ESEM[e])
        for e in ENGINES:
            ws = [(ESEM[o], marks[o]) for o in ENGINES if o != e]
            self.emit(e, lambda en: en.nop(), waits=ws)



def build_program():
    import os
    rd = os.environ.get("K_RACEDET", "0") == "1"
    nc = bass.Bass("TRN2", target_bir_lowering=False, detect_race_conditions=rd)
    io = {}

    def ein(name, shape, dt):
        io[name] = nc.dram_tensor(name, shape, dt, kind="ExternalInput").ap()

    def eout(name, shape, dt):
        io[name] = nc.dram_tensor(name, shape, dt, kind="ExternalOutput").ap()

    def ind(name, shape, dt):
        io[name] = nc.dram_tensor(name, shape, dt).ap()

    ein("resid_in", [D, TOWN], F32)
    ein("wq_s", [4, 128, DC, 128], F32)
    ein("wk_s", [1, 128, DC, 128], F32)
    ein("wv_s", [1, 128, DC, 128], F32)
    ein("wo_s", [DC, 128, DC, 128], F32)
    ein("upc", [UPS * 128, D], I32)
    ein("upam", [128, UPS * BLK], F32)
    ein("dnc", [512, H], I32)
    ein("dnam", [128, 4 * (H // BLK)], F32)
    ein("bup", [128, HC], F32)
    ein("bdn", [128, DC], F32)
    ein("aupT", [16, D], F32)
    ein("adnT", [16, H], F32)
    ein("bu_sl", [16, UPS * 128], F32)
    ein("bd_sl", [16, 512], F32)
    ein("ropec", [128, T], F32)
    ein("ropes", [128, T], F32)
    ein("masks", [128, 4 * TT], BF16)
    ein("ident", [128, 128], F32)

    eout("out_d", [D, TOWN], F32)

    ind("x_mid", [D, TOWN], F32)
    ind("x_attn", [D, TOWN], F32)
    ind("wup_part", [UPS, 128, DC, 128], BF16)
    io["wup_all"] = nc.dram_tensor("wup_all", [NC * UPS, 128, DC, 128], BF16,
                                   addr_space="Shared").ap()
    ind("wdn_part", [2, 4, 128, 43, 128], BF16)
    io["wdn_all"] = nc.dram_tensor("wdn_all", [2, 4, NC, 128, 43, 128], BF16,
                                   addr_space="Shared").ap()
    ind("xh_part", [NTT, 128, DC, TT], BF16)
    io["xh_all"] = nc.dram_tensor("xh_all", [NC * NTT, 128, DC, TT], BF16,
                                   addr_space="Shared").ap()
    ind("ctx_part", [NC, 4, 128, TOWN], BF16)
    ind("ctx_own", [NC, 4, 128, TOWN], BF16)
    ind("q_dram", [4, 128, T], BF16)
    ind("dbg1", [128, DC * TT], BF16)
    ind("dbg2", [128, 43 * TT], BF16)

    stack = contextlib.ExitStack()
    sb = {}

    def sbuf(name, shape, dt):
        sb[name] = stack.enter_context(nc.sbuf_tensor(name, shape, dt))

    def psum(name, shape, dt):
        sb[name] = stack.enter_context(nc.psum_tensor(name, shape, dt))

    with stack:
        sbuf("xh", [128, DC * TT], BF16)
        sbuf("x2", [128, 43 * TT], BF16)
        sbuf("karena", [128, T], BF16)
        sbuf("varena", [128, T], BF16)
        sbuf("wsb", [128, 11008], BF16)
        sbuf("dq_w", [128, 2 * 1024], I32)
        sbuf("dq_stg", [128, 1024], BF16)
        sbuf("dq_amu", [128, 704], F32)
        sbuf("dq_amd", [128, 688], F32)
        sbuf("dq_bu", [16, 128], BF16)
        sbuf("dq_at", [16, 1024], BF16)
        sbuf("rc", [128, 2 * TT], F32)
        sbuf("rs", [128, 2 * TT], F32)
        sbuf("mk", [128, 4 * TT], BF16)
        sbuf("idn", [128, 128], F32)
        sbuf("bup_sb", [128, HC], F32)
        sbuf("bdn_sb", [128, DC], F32)
        sbuf("bdnh", [128, DC], F32)
        sbuf("ones_c", [128, 1], BF16)
        sbuf("ones_rf", [1, 128], F32)
        sbuf("ones_r16", [1, 128], FP16)
        sbuf("r16", [1, 512], FP16)
        sbuf("xs", [128, 2 * TT], F32)
        sbuf("xbf", [128, TT], BF16)
        sbuf("xsq", [128, TT], BF16)
        sbuf("rows", [1, 4 * TT], F32)
        sbuf("mb", [128, TT], F32)
        sbuf("rb_", [128, TT], F32)
        sbuf("va", [128, TT], F32)
        sbuf("vb", [128, TT], F32)
        sbuf("eb", [128, 2 * 2048], BF16)
        sbuf("em", [128, 2048], BF16)
        sb["dq_ct"] = sb["dq_w"].ap()
        sb["dq_stt"] = bass.AP(sb["karena"].ap().tensor.bitcast(F32), 0,
                               [[4096, 128], [1, 1024]])
        sb["dq_cf"] = bass.AP(sb["eb"].ap().tensor.bitcast(FP16), 0,
                              [[4096, 128], [1, 2048]])
        sb["dq_acc"] = bass.AP(sb["em"].ap().tensor.bitcast(FP16), 0,
                               [[2048, 128], [1, 1024]])
        sb["dq_trm"] = bass.AP(sb["em"].ap().tensor.bitcast(FP16), 1024,
                               [[2048, 128], [1, 1024]])
        sbuf("lb", [128, TT], F32)
        sbuf("qd", [128, 2 * TT], BF16)
        sbuf("cstg", [128, TOWN], BF16)
        sbuf("ostg", [128, 2 * TT], F32)
        psum("psA", [128, 2048], F32)
        psum("ps4", [128, 512], F32)
        psum("ps5", [128, 512], F32)
        psum("ps6", [128, 512], F32)
        psum("ps7", [128, 512], F32)
        for i in range(4):
            sb[f"ps{i}"] = sb["psA"].ap()[:, i * 512:(i + 1) * 512]

        semstack = contextlib.ExitStack()
        sems = {}
        with semstack, nc.Block() as block:
            for name in SEM_NAMES:
                sems[name] = semstack.enter_context(nc.semaphore("s_" + name))

            def build_all(cur):
                p = P(nc, sems, cur)
                emit_program(p, io, sb)

            @block.sync
            def _(eng):
                build_all("sync")

            @block.scalar
            def _(eng):
                build_all("scalar")

            @block.vector
            def _(eng):
                build_all("vector")

            @block.tensor
            def _(eng):
                build_all("tensor")

            @block.gpsimd
            def _(eng):
                build_all("gpsimd")

    return nc


def phase_ffn1(p, io, sb):
    DQ_COMMON["side"] = None
    st_up = dq_state()
    st_dn = dq_state()

    def dn_half(hh):
        for dcl in range(4):
            dq_run(p, io, sb, st_dn, "dn",
                   [(dcl, hh, pc) for pc in range(6)])
            dq_dn_gather(p, io, sb, hh, dcl, p.val("stg"))

    def hook(tt, half, stage, i):
        if tt != 0:
            return
        if (half, stage) == (0, "up"):
            dq_up_slabs(p, io, sb, st_up, 0, 2)
        elif stage == "up_hc":
            hc = half * 43 + i
            if hc % 8 == 0 and hc // 8 in st_up.get("stg_r", {}):
                dq_up_gather(p, io, sb, st_up, hc // 8)
            if half == 0 and i in (8, 16, 24, 32):
                dq_up_slabs(p, io, sb, st_up, i // 8 + 1, i // 8 + 2)
        elif (half, stage) == (0, "down"):
            dn_half(0)
        elif (half, stage) == (0, "down_dc") and i in (5, 11, 17, 23, 29):
            s = 6 + (5, 11, 17, 23, 29).index(i)
            dq_up_slabs(p, io, sb, st_up, s, s + 1)
            dq_up_gather(p, io, sb, st_up, s)
        elif (half, stage) == (1, "down"):
            dn_half(1)

    phase_ffn(p, io, sb, src="resid_in", dst="x_mid", hook=hook)


def emit_program(p, io, sb):
    import os
    np_ = int(os.environ.get("K_PHASES", "99"))
    phases = [
        lambda: phase_consts(p, io, sb),
        lambda: phase_ffn1(p, io, sb),
        lambda: phase_ln2qkv(p, io, sb),
        lambda: phase_attn(p, io, sb),
        lambda: phase_ffn(p, io, sb, src="x_attn", dst="out_d"),
    ]
    np_ += 1
    for i, ph in enumerate(phases):
        if i >= np_:
            break
        ph()
        p.barrier()
    dump = os.environ.get("K_DUMP", "")
    if dump:
        if ":" in dump:
            name, off = dump.split(":")
            off = int(off)
            nel = D * TOWN * 2
            tsz = 1
            for s_ in io[name].tensor.shape:
                tsz *= s_
            bpe = {BF16: 2, F32: 4, I32: 4}[io[name].tensor.dtype]
            nel = min(nel, tsz * bpe // 2)
            flat = bass.AP(io[name].tensor.bitcast(BF16), off, [[1, nel]])
            dstv = bass.AP(io["out_d"].tensor.bitcast(BF16), 0, [[1, nel]])
            p.emit("gpsimd", lambda e, s_=flat, d_=dstv: e.dma_start(out=d_, in_=s_),
                   inc="stA", by=16)
        else:
            p.emit("gpsimd", lambda e: e.dma_start(out=io["out_d"][:, :],
                                                   in_=io[dump][:, :]),
                   inc="stA", by=16)
    p.emit("gpsimd", lambda e: e.nop(),
           waits=[(s, p.val(s)) for s in GP_STREAMS])


def pe_fence(p, sb):
    p.emit("tensor",
           lambda e: e.matmul(sb["ps5"][0:1, 0:1], sb["ones_c"][:, :],
                              sb["ones_c"][:, :], start=True, stop=True),
           inc="pe")



def phase_consts(p, io, sb):
    p.emit("sync", lambda e: e.dma_start(out=sb["idn"][:, :], in_=io["ident"][:, :]),
           inc="cst", by=16)
    p.emit("sync", lambda e: e.dma_start(out=sb["mk"][:, :], in_=io["masks"][:, :]),
           inc="cst", by=16)
    p.emit("sync", lambda e: e.dma_start(out=sb["bup_sb"][:, :], in_=io["bup"][:, :]),
           inc="cst", by=16)
    p.emit("sync", lambda e: e.dma_start(out=sb["bdn_sb"][:, :], in_=io["bdn"][:, :]),
           inc="cst", by=16)
    p.emit("sync", lambda e: e.dma_start(out=sb["dq_amu"][:, :], in_=io["upam"][:, 0:704]),
           inc="cst", by=16)
    p.emit("sync", lambda e: e.dma_start(out=sb["dq_amd"][:, :], in_=io["dnam"][:, :]),
           inc="cst", by=16)
    p.emit("vector", lambda e: e.memset(sb["ones_c"][:, :], 1.0), inc="dve")
    p.emit("vector", lambda e: e.memset(sb["ones_rf"][:, :], 1.0), inc="dve")
    p.emit("vector", lambda e: e.memset(sb["ones_r16"][:, :], 1.0), inc="dve")
    p.emit("vector",
           lambda e: e.tensor_scalar(sb["bdnh"][:, :], sb["bdn_sb"][:, :], 0.5,
                                     None, A.mult),
           waits=[("cst", p.val("cst"))], inc="dve")



def dq_state():
    return dict(act_mark={}, dve_mark={}, stg_free={}, pe_mark={},
                lora_ld={}, i=0)


DQ_COMMON = {"side": None}


def dq_run(p, io, sb, st, side, tiles):
    nf4 = NF4
    ct, cf = sb["dq_ct"], sb["dq_cf"]
    codes = io["upc"] if side == "up" else io["dnc"]
    act_mark = st["act_mark"]
    dve_mark = st["dve_mark"]
    stg_free = st["stg_free"]
    pe_mark = st["pe_mark"]
    lora_ld = st["lora_ld"]
    amtab = sb["dq_amu"] if side == "up" else sb["dq_amd"]
    if DQ_COMMON["side"] != side:
        DQ_COMMON["side"] = side
        ent = {"act": p.val("act"), "pe": p.val("pe"),
               "dve": p.val("dve"), "stg": p.val("stg")}
    else:
        ent = {"act": 0, "pe": 0, "dve": 0, "stg": 0}
    kk = 0
    first_tile = True

    for tile in tiles:
        i = st["i"]
        st["i"] += 1
        if side == "up":
            s, q = tile
            F = 1024
            src = codes[s * 128:(s + 1) * 128, q * 1024:q * 1024 + F]
        else:
            dcl, hh, pc = tile
            F = 1024 if pc < 5 else 384
            col0 = hh * 5504 + pc * 1024
            src = codes[dcl * 128:(dcl + 1) * 128, col0:col0 + F]
        nj = F // 128
        if True:
            for hi, h2 in enumerate(range(0, F, 1024)):
                fh = min(1024, F - h2)
                sidx = i + hi
                cbuf = sidx % 2
                cdsem = "cdA" if cbuf == 0 else "cdB"
                p.emit("sync",
                       lambda e, o=ct[:, cbuf * 1024:cbuf * 1024 + fh],
                       s_=src[:, h2:h2 + fh]: e.dma_start(out=o, in_=s_),
                       waits=[("dve", dve_mark.get(("cast", sidx - 2), 0))],
                       inc=cdsem, by=16)
                ldm = (cdsem, p.val(cdsem))
                p.emit("vector",
                       lambda e, o=cf[:, cbuf * 1024:cbuf * 1024 + fh],
                       s_=ct[:, cbuf * 1024:cbuf * 1024 + fh]:
                       e.tensor_copy(o, s_),
                       waits=[ldm], inc="dve")
                dve_mark[("cast", sidx)] = p.val("dve")
                cfv = cf[:, cbuf * 1024:cbuf * 1024 + fh]
                accv = sb["dq_acc"][:, 0:fh]
                trmv = sb["dq_trm"][:, 0:fh]
                p.emit("vector",
                       lambda e, o=accv, s_=cfv: e.tensor_scalar(
                           o, s_, 0.0, float(nf4[0]), A.is_equal, A.mult),
                       inc="dve")
                for v in range(1, 16):
                    p.emit("vector",
                           lambda e, o=trmv, s_=cfv, vv=v: e.tensor_scalar(
                               o, s_, float(vv), float(nf4[vv]), A.is_equal, A.mult),
                           inc="dve")
                    p.emit("vector",
                           lambda e, o=accv, t_=trmv: e.tensor_tensor(o, o, t_, A.add),
                           inc="dve")
                dve_mark[("acc", sidx)] = p.val("dve")
                nb = fh // BLK
                if side == "up":
                    off = s * BLK + (q * 1024 + h2) // BLK
                else:
                    off = dcl * (H // BLK) + (col0 + h2) // BLK
                amv = amtab[:, off:off + nb]
                am_b = amv.rearrange("p (b one) -> p b one", one=1).broadcast_to(
                    [128, nb, BLK])
                accb = accv.rearrange("p (b r) -> p b r", r=BLK)
                sttv = sb["dq_stt"][:, h2:h2 + fh].rearrange("p (b r) -> p b r", r=BLK)
                p.emit("vector",
                       lambda e, o=sttv, s_=accb, m=am_b: e.scalar_tensor_tensor(
                           o, s_, 0.0, m, A.add, A.mult),
                       waits=[("pe", max(pe_mark.get(("tr", i - 1), 0),
                                         ent["pe"] if kk < 2 else 0))],
                       inc="dve")
                dve_mark[("stt", sidx)] = p.val("dve")
                kk += 1
            stt_done = p.val("dve")

            if side == "up":
                at_src = io["aupT"][:, q * 1024:q * 1024 + F]
                b_src = io["bu_sl"][:, s * 128:(s + 1) * 128]
            else:
                at_src = io["adnT"][:, col0:col0 + F]
                b_src = io["bd_sl"][:, dcl * 128:(dcl + 1) * 128]
            p.emit("gpsimd",
                   lambda e, s_=at_src, f=F: e.dma_start(out=sb["dq_at"][:, 0:f], in_=s_),
                   waits=[("pe", max(pe_mark.get(("tr", i - 1), 0),
                                     ent["pe"] if first_tile else 0))],
                   inc="lra", by=16)
            p.emit("gpsimd",
                   lambda e, s_=b_src: e.dma_start(out=sb["dq_bu"][:, :], in_=s_),
                   inc="lra", by=16)
            lora_ld[i] = p.val("lra")

            wv = [("dve", stt_done), ("lra", lora_ld[i])]
            for j in range(nj):
                bank = sb["ps6"] if j < 4 else sb["ps7"]
                col = (j % 4) * 128
                bfirst = (j % 4 == 0)
                blast = (j == nj - 1) or (j % 4 == 3)
                p.emit("tensor",
                       lambda e, b_=bank, c=col, j_=j, f=bfirst: e.matmul(
                           b_[:, c:c + 128], sb["dq_stt"][:, j_ * 128:(j_ + 1) * 128],
                           sb["idn"][:, :], start=f, stop=False, is_transpose=True),
                       waits=wv if j == 0 else (), inc="pe")
                p.emit("tensor",
                       lambda e, b_=bank, c=col, j_=j, l_=blast: e.matmul(
                           b_[:, c:c + 128], sb["dq_at"][:, j_ * 128:(j_ + 1) * 128],
                           sb["dq_bu"][:, :], start=False, stop=l_),
                       inc="pe")
            pe_mark[("tr", i)] = p.val("pe")

            sgb = 0
            p.emit("vector",
                   lambda e, o=sb["dq_stg"][:, sgb:sgb + min(nj, 4) * 128]:
                   e.tensor_copy(o, sb["ps6"][:, 0:min(nj, 4) * 128]),
                   waits=[("pe", pe_mark[("tr", i)]),
                          ("stg", max(stg_free.get(i - 1, 0),
                                      ent["stg"] if first_tile else 0))],
                   inc="dve")
            if nj > 4:
                p.emit("vector",
                       lambda e, o=sb["dq_stg"][:, sgb + 512:sgb + 512 + (nj - 4) * 128]:
                       e.tensor_copy(o, sb["ps7"][:, 0:(nj - 4) * 128]),
                       inc="dve")
            act_mark[("evac", i)] = p.val("dve")

            if side == "up":
                dst = io["wup_part"][s, :, 8 * q:8 * q + nj, :].rearrange(
                    "p c l -> p (c l)")
            else:
                dst = io["wdn_part"][hh, dcl, :, 8 * pc:8 * pc + nj, :].rearrange(
                    "p c l -> p (c l)")
            p.emit("gpsimd",
                   lambda e, d=dst, o=sb["dq_stg"][:, sgb:sgb + nj * 128]:
                   e.dma_start(out=d, in_=o),
                   waits=[("dve", act_mark[("evac", i)])], inc="stg", by=16)
            stg_free[i] = p.val("stg")
            first_tile = False


def dq_up_slabs(p, io, sb, st, s0, s1):
    for s in range(s0, s1):
        dq_run(p, io, sb, st, "up", [(s, q) for q in range(4)])
        st.setdefault("stg_r", {})[s] = p.val("stg")


def dq_up_gather(p, io, sb, st, r):
    if r in st.setdefault("g_done", set()):
        return
    st["g_done"].add(r)
    p.emit("gpsimd",
           lambda e, s_=r: e.collective_compute(
               "AllGather", A.bypass, replica_groups=[list(range(NC))],
               ins=[io["wup_part"][s_].rearrange("p c l -> p (c l)")],
               outs=[io["wup_all"][8 * s_:8 * s_ + 8]
                     .rearrange("s p c l -> (s p) (c l)")]),
           waits=[("stg", st["stg_r"][r])], inc="ccu")


def dq_dn_gather(p, io, sb, hh, dcl, stg_mark):
    p.emit("gpsimd",
           lambda e, h=hh, d=dcl: e.collective_compute(
               "AllGather", A.bypass, replica_groups=[list(range(NC))],
               ins=[io["wdn_part"][h, d].rearrange("p c l -> p (c l)")],
               outs=[io["wdn_all"][h, d].rearrange("s p c l -> (s p) (c l)")]),
           waits=[("stg", stg_mark)], inc="ccd")



def snap(p):
    return {s: p.val(s) for s in ("pe", "act", "dve", "stA", "stB")}


def emit_ln(p, io, sb, src_ap, tt, lnpart=9, carry=None):
    if carry is None:
        carry = snap(p)
    ps_sum, ps_sq = sb["ps4"], sb["ps5"]
    xs = sb["xs"]
    ld = {}
    cast_m = {}
    pe_m = {}
    for dc in range(DC):
        buf = dc % 2
        xsem = "xsA" if buf == 0 else "xsB"
        p.emit("sync",
               lambda e, b=buf, d=dc: e.dma_start(
                   out=xs[:, b * TT:(b + 1) * TT],
                   in_=src_ap[d * 128:(d + 1) * 128, tt * TT:(tt + 1) * TT]),
               waits=[("act", cast_m.get(dc - 2, 0)),
                      ("dve", carry["dve"] if dc < 2 else 0)],
               inc=xsem, by=16)
        ld[dc] = (xsem, p.val(xsem))
        p.emit("scalar",
               lambda e, b=buf: e.copy(sb["xbf"][:, :], xs[:, b * TT:(b + 1) * TT]),
               waits=[ld[dc], ("pe", pe_m.get(dc - 1, carry["pe"] if dc < 1 else 0))],
               inc="act")
        p.emit("scalar",
               lambda e, b=buf: e.activation(
                   out=sb["xsq"][:, :], in_=xs[:, b * TT:(b + 1) * TT],
                   func=AF.Square),
               inc="act")
        cast_m[dc] = p.val("act")
        p.emit("tensor",
               lambda e, d=dc: e.matmul(ps_sum[0:1, :], sb["ones_c"][:, :],
                                        sb["xbf"][:, :], start=(d == 0),
                                        stop=(d == DC - 1)),
               waits=[("act", cast_m[dc])], inc="pe")
        p.emit("tensor",
               lambda e, d=dc: e.matmul(ps_sq[0:1, :], sb["ones_c"][:, :],
                                        sb["xsq"][:, :], start=(d == 0),
                                        stop=(d == DC - 1)),
               inc="pe")
        pe_m[dc] = p.val("pe")
    stats_pe = p.val("pe")
    stats_act = p.val("act")
    if lnpart <= 1:
        return p.val("dve")
    rows = sb["rows"]
    m_row = rows[0:1, 0 * TT:1 * TT]
    s_row = rows[0:1, 1 * TT:2 * TT]
    v_row = rows[0:1, 2 * TT:3 * TT]
    r_row = rows[0:1, 3 * TT:4 * TT]
    p.emit("scalar",
           lambda e: e.activation(out=m_row, in_=ps_sum[0:1, :], func=AF.Copy,
                                  bias=0.0, scale=1.0 / D),
           waits=[("pe", stats_pe)], inc="act")
    p.emit("scalar",
           lambda e: e.activation(out=s_row, in_=ps_sq[0:1, :], func=AF.Copy,
                                  bias=0.0, scale=1.0 / D),
           inc="act")
    p.emit("vector", lambda e: e.tensor_tensor(v_row, m_row, m_row, A.mult),
           waits=[("act", p.val("act"))], inc="dve")
    p.emit("vector", lambda e: e.tensor_tensor(v_row, s_row, v_row, A.subtract),
           inc="dve")
    p.emit("vector",
           lambda e: e.tensor_scalar(v_row, v_row, EPS, None, A.add),
           inc="dve")
    p.emit("scalar",
           lambda e: e.activation(out=s_row, in_=v_row, func=AF.Sqrt),
           waits=[("dve", p.val("dve"))], inc="act")
    d_row = rows[0:1, 2 * TT:3 * TT]
    p.emit("scalar",
           lambda e, d=d_row: e.copy(d, s_row),
           inc="act")
    p.emit("vector", lambda e: e.reciprocal(r_row, s_row),
           waits=[("act", p.val("act"))], inc="dve")
    if lnpart <= 2:
        return p.val("dve")
    p.emit("tensor",
           lambda e: e.matmul(sb["ps6"][:, :], sb["ones_rf"][:, :], m_row,
                              start=True, stop=True),
           waits=[("dve", p.val("dve")), ("act", carry["act"])], inc="pe")
    p.emit("tensor",
           lambda e: e.matmul(sb["ps7"][:, :], sb["ones_rf"][:, :], r_row,
                              start=True, stop=True),
           inc="pe")
    bc_pe = p.val("pe")
    pe_fence(p, sb)
    if lnpart <= 3:
        return p.val("dve")
    p.emit("vector", lambda e: e.tensor_copy(sb["mb"][:, :], sb["ps6"][:, :]),
           waits=[("pe", p.val("pe")), ("dve", p.val("dve"))], inc="dve")
    p.emit("vector", lambda e: e.tensor_copy(sb["rb_"][:, :], sb["ps7"][:, :]),
           inc="dve")
    bc_dve = p.val("dve")
    if lnpart <= 4:
        return p.val("dve")
    ap_m = {}
    for dc in range(DC):
        buf = dc % 2
        xsem = "xsA" if buf == 0 else "xsB"
        w = [("dve", ap_m.get(dc - 2, 0))]
        if dc < 2:
            w.append(("act", stats_act))
        p.emit("sync",
               lambda e, b=buf, d=dc: e.dma_start(
                   out=xs[:, b * TT:(b + 1) * TT],
                   in_=src_ap[d * 128:(d + 1) * 128, tt * TT:(tt + 1) * TT]),
               waits=w, inc=xsem, by=16)
        mark = (xsem, p.val(xsem))
        p.emit("vector",
               lambda e, b=buf: e.tensor_tensor(sb["va"][:, :],
                                                xs[:, b * TT:(b + 1) * TT],
                                                sb["mb"][:, :], A.subtract),
               waits=[mark, ("dve", bc_dve)], inc="dve")
        p.emit("vector",
               lambda e, d=dc: e.tensor_tensor(
                   sb["xh"][:, d * TT:(d + 1) * TT], sb["va"][:, :],
                   sb["rb_"][:, :], A.mult),
               inc="dve")
        ap_m[dc] = p.val("dve")
    return p.val("dve")



def phase_ffn(p, io, sb, src, dst, hook=None):
    final = (dst == "out_d")
    src_ap = io[src]
    inter = io["x_mid"]
    wsb = sb["wsb"]
    for tt in range(NTT):
        ln_done = emit_ln(p, io, sb, src_ap, tt)
        for half in range(2):
            if hook:
                hook(tt, half, "up", 0)
            cu = snap(p)
            relu_m = {}
            up_pe = {}
            slab_ld = {}
            for hcl in range(43):
                hc = half * 43 + hcl
                if hook:
                    hook(tt, half, "up_hc", hcl)
                buf = hcl % 2
                wsem = "wsA" if buf == 0 else "wsB"
                p.emit("sync",
                       lambda e, b=buf, s=hc: e.dma_start(
                           out=wsb[:, b * D:(b + 1) * D],
                           in_=io["wup_all"][s].rearrange("p c l -> p (c l)")),
                       waits=[("pe", up_pe.get(hcl - 2, cu["pe"])),
                              ("ccu", hc // 8 + 1)],
                       inc=wsem, by=16)
                slab_ld[hcl] = (wsem, p.val(wsem))
                bank = sb[f"ps{hcl % 2}"]
                for dc in range(DC):
                    w = []
                    if dc == 0:
                        w = [slab_ld[hcl],
                             ("act", relu_m.get(hcl - 2, cu["act"])),
                             ("dve", ln_done)]
                    p.emit("tensor",
                           lambda e, b=bank, bf=buf, d=dc: e.matmul(
                               b[:, :],
                               wsb[:, bf * D + d * 128:bf * D + (d + 1) * 128],
                               sb["xh"][:, d * TT:(d + 1) * TT],
                               start=(d == 0), stop=(d == DC - 1)),
                           waits=w, inc="pe")
                up_pe[hcl] = p.val("pe")
                p.emit("scalar",
                       lambda e, b=bank, h=hc, hl=hcl: e.activation(
                           out=sb["x2"][:, hl * TT:(hl + 1) * TT], in_=b[:, :],
                           func=AF.Relu, bias=sb["bup_sb"][:, h:h + 1], scale=1.0),
                       waits=[("pe", up_pe[hcl])], inc="act")
                relu_m[hcl] = p.val("act")
            up_done_act = p.val("act")
            up_pe_last = p.val("pe")
            if hook:
                hook(tt, half, "down", 0)
            cd = snap(p)
            dn_ld = {}
            dn_pe = {}
            dn_ev = {}
            dn_st = {}
            rd = src_ap if half == 0 else inter
            wr = inter if half == 0 else io[dst]
            for di, dc in enumerate(dcl + 4 * k
                                    for dcl in range(4) for k in range(8)):
                if hook:
                    hook(tt, half, "down_dc", di)
                buf = di % 2
                wsem = "wsA" if buf == 0 else "wsB"
                xsem = "xsA" if buf == 0 else "xsB"
                ccd_w = ("ccd", half * 4 + dc % 4 + 1)
                wv = [("pe", dn_pe.get(di - 2, 0)), ccd_w]
                if di < 2:
                    wv = [("pe", up_pe_last), ccd_w]
                p.emit("sync",
                       lambda e, b=buf, d=dc, h=half: e.dma_start(
                           out=wsb[:, b * 5504:(b + 1) * 5504],
                           in_=io["wdn_all"][h, d % 4, d // 4]
                           .rearrange("p c l -> p (c l)")),
                       waits=wv, inc=wsem, by=16)
                dn_ld[di] = (wsem, p.val(wsem))
                wv = [("dve", dn_ev.get(di - 2, max(ln_done, cd["dve"])
                                        if di < 2 else 0))]
                if half == 1 and di < 2:
                    wv += [("stA", cd["stA"]), ("stB", cd["stB"])]
                p.emit("sync",
                       lambda e, b=buf, d=dc, r=rd: e.dma_start(
                           out=sb["xs"][:, b * TT:(b + 1) * TT],
                           in_=r[d * 128:(d + 1) * 128, tt * TT:(tt + 1) * TT]),
                       waits=wv, inc=xsem, by=16)
                xs_ld = (xsem, p.val(xsem))
                bank = sb[f"ps{2 + di % 2}"]
                for hcl in range(43):
                    w = []
                    if hcl == 0:
                        w = [dn_ld[di], ("act", up_done_act),
                             ("dve", dn_ev.get(di - 2, cd["dve"]))]
                    last = (hcl == 42)
                    p.emit("tensor",
                           lambda e, b=bank, bf=buf, hl=hcl, l_=last: e.matmul(
                               b[:, :],
                               wsb[:, bf * 5504 + hl * 128:bf * 5504 + (hl + 1) * 128],
                               sb["x2"][:, hl * TT:(hl + 1) * TT],
                               start=(hl == 0), stop=l_),
                           waits=w, inc="pe")
                dn_pe[di] = p.val("pe")
                ob = di % 2
                wprev = dn_st.get(di - 2,
                                  ("stA" if ob == 0 else "stB",
                                   cd["stA" if ob == 0 else "stB"]))
                w = [("pe", dn_pe[di]), xs_ld, wprev]
                if half == 1:
                    p.emit("vector",
                           lambda e, b=bank, d=dc: e.tensor_scalar(
                               sb["va"][:, :], b[:, :], 0.5,
                               sb["bdnh"][:, d:d + 1], A.mult, A.add),
                           waits=w, inc="dve")
                    p.emit("vector",
                           lambda e, bf=buf, o=ob: e.tensor_tensor(
                               sb["ostg"][:, o * TT:(o + 1) * TT], sb["va"][:, :],
                               sb["xs"][:, bf * TT:(bf + 1) * TT], A.add),
                           inc="dve")
                else:
                    p.emit("vector",
                           lambda e, b=bank, bf=buf, o=ob: e.scalar_tensor_tensor(
                               sb["ostg"][:, o * TT:(o + 1) * TT], b[:, :], 0.5,
                               sb["xs"][:, bf * TT:(bf + 1) * TT], A.mult, A.add),
                           waits=w, inc="dve")
                dn_ev[di] = p.val("dve")
                ssem = "stA" if ob == 0 else "stB"
                p.emit("gpsimd",
                       lambda e, d=dc, o=ob, w_=wr: e.dma_start(
                           out=w_[d * 128:(d + 1) * 128, tt * TT:(tt + 1) * TT],
                           in_=sb["ostg"][:, o * TT:(o + 1) * TT]),
                       waits=[("dve", dn_ev[di])], inc=ssem, by=16)
                dn_st[di] = (ssem, p.val(ssem))



def phase_ln2qkv(p, io, sb):
    wsb = sb["wsb"]
    for oc in range(6):
        if oc < 4:
            wsrc, dst = io["wq_s"][oc], sb["x2"][:, oc * D:(oc + 1) * D]
        elif oc == 4:
            wsrc, dst = io["wk_s"][0], wsb[:, 0:D]
        else:
            wsrc, dst = io["wv_s"][0], wsb[:, D:2 * D]
        p.emit("gpsimd",
               lambda e, s_=wsrc, d_=dst: e.dma_start(
                   out=d_, in_=s_.rearrange("p c l -> p (c l)")),
               inc="wgA", by=16)
    slab_ld = ("wgA", p.val("wgA"))

    for tt in range(NTT):
        if tt == 1:
            p.emit("vector", lambda e: e.nop(),
                   waits=[("xps", p.val("xps"))])
        ln_done = emit_ln(p, io, sb, io["x_mid"], tt)
        for dc in range(DC):
            p.emit("gpsimd",
                   lambda e, d=dc, t_=tt: e.dma_start(
                       out=io["xh_part"][t_, :, d, :],
                       in_=sb["xh"][:, d * TT:(d + 1) * TT]),
                   waits=[("dve", ln_done)], inc="xps", by=16)
        p.emit("gpsimd",
               lambda e, t_=tt: e.collective_compute(
                   "AllGather", A.bypass, replica_groups=[list(range(NC))],
                   ins=[io["xh_part"][t_].rearrange("p c t -> p (c t)")],
                   outs=[io["xh_all"][t_ * NC:(t_ + 1) * NC]
                         .rearrange("a p c t -> (a p) (c t)")]),
               waits=[("xps", p.val("xps"))], inc="ccx")
    xps_all = p.val("xps")

    def slabv(oc, d):
        if oc < 4:
            return sb["x2"][:, oc * D + d * 128:oc * D + (d + 1) * 128]
        return wsb[:, (oc - 4) * D + d * 128:(oc - 4) * D + (d + 1) * 128]

    wld = {}
    ev = {}
    for tg, (ttl, rank) in enumerate(
            (t, r) for t in range(NTT) for r in range(NC)):
        gt = rank * NTT + ttl
        w = [("pe", ev.get(("pe_last", tg - 1), 0)), ("ccx", ttl + 1)]
        if tg < 2:
            w.append(("xps", xps_all))
        p.emit("sync",
               lambda e, t_=ttl, r=rank: e.dma_start(
                   out=sb["xh"][:, :],
                   in_=io["xh_all"][t_ * NC + r].rearrange("p c t -> p (c t)")),
               waits=w, inc="xhl", by=16)
        xh_ld = ("xhl", p.val("xhl"))
        rbuf = tg % 2
        rpsem = "rpA" if rbuf == 0 else "rpB"
        p.emit("sync",
               lambda e, b=rbuf, t_=gt: e.dma_start(
                   out=sb["rc"][:, b * TT:(b + 1) * TT],
                   in_=io["ropec"][:, t_ * TT:(t_ + 1) * TT]),
               waits=[("dve", ev.get(("rope_dve", tg - 2), 0))], inc=rpsem, by=16)
        p.emit("sync",
               lambda e, b=rbuf, t_=gt: e.dma_start(
                   out=sb["rs"][:, b * TT:(b + 1) * TT],
                   in_=io["ropes"][:, t_ * TT:(t_ + 1) * TT]),
               inc=rpsem, by=16)
        rope_ld = (rpsem, p.val(rpsem))
        for oc in range(6):
            gi = tg * 6 + oc
            bank = sb[f"ps{oc % 2}"]
            for dc in range(DC):
                w = []
                if dc == 0:
                    w = [slab_ld, xh_ld,
                         ("dve", ev.get(("cons", gi - 2), 0)),
                         ("act", ev.get(("vcons", gi - 2), 0))]
                p.emit("tensor",
                       lambda e, o=oc, d=dc, b=bank: e.matmul(
                           b[:, :], slabv(o, d),
                           sb["xh"][:, d * TT:(d + 1) * TT],
                           start=(d == 0), stop=(d == DC - 1)),
                       waits=w, inc="pe")
            wld[gi] = p.val("pe")
            mmdone = p.val("pe")
            if oc < 5:
                rcv = sb["rc"][:, rbuf * TT:(rbuf + 1) * TT]
                rsv = sb["rs"][:, rbuf * TT:(rbuf + 1) * TT]
                p.emit("vector",
                       lambda e, b=bank, r=rcv: e.tensor_tensor(
                           sb["va"][:, :], b[:, :], r, A.mult),
                       waits=[("pe", mmdone), rope_ld], inc="dve")
                p.emit("vector",
                       lambda e, b=bank, r=rsv: e.tensor_tensor(
                           sb["vb"][0:64, :], b[64:128, :], r[0:64, :], A.mult),
                       inc="dve")
                p.emit("vector",
                       lambda e, b=bank, r=rsv: e.tensor_tensor(
                           sb["vb"][64:128, :], b[0:64, :], r[64:128, :], A.mult),
                       inc="dve")
                ev[("cons", gi)] = p.val("dve")
                if oc < 4:
                    qb2 = (tg * 4 + oc) % 2
                    dstv = sb["qd"][:, qb2 * TT:(qb2 + 1) * TT]
                    qssem = "qsA" if qb2 == 0 else "qsB"
                    p.emit("vector",
                           lambda e, o=dstv: e.tensor_tensor(
                               o, sb["va"][:, :], sb["vb"][:, :], A.add),
                           waits=[ev.get(("qst", tg * 4 + oc - 2), (qssem, 0))],
                           inc="dve")
                    qdve = p.val("dve")
                    p.emit("gpsimd",
                           lambda e, o=dstv, h=oc, t_=gt: e.dma_start(
                               out=io["q_dram"][h, :, t_ * TT:(t_ + 1) * TT], in_=o),
                           waits=[("dve", qdve)], inc=qssem, by=16)
                    ev[("qst", tg * 4 + oc)] = (qssem, p.val(qssem))
                else:
                    p.emit("vector",
                           lambda e, t_=gt: e.tensor_tensor(
                               sb["karena"][:, t_ * TT:(t_ + 1) * TT],
                               sb["va"][:, :], sb["vb"][:, :], A.add),
                           inc="dve")
                ev[("rope_dve", tg)] = p.val("dve")
                ev[("cons", gi)] = p.val("dve")
            else:
                p.emit("vector",
                       lambda e, b=bank: e.tensor_copy(sb["va"][:, :], b[:, :]),
                       waits=[("pe", mmdone),
                              ("pe", ev.get(("vtr", tg - 1), 0))], inc="dve")
                ev[("cons", gi)] = p.val("dve")
                vdve = p.val("dve")
                for j in range(4):
                    b2 = sb["ps6"] if j < 2 else sb["ps7"]
                    col = (j % 2) * 128
                    p.emit("tensor",
                           lambda e, bb=b2, c=col, j_=j: e.matmul(
                               bb[:, c:c + 128],
                               sb["va"][:, j_ * 128:(j_ + 1) * 128],
                               sb["idn"][:, :], start=(j_ % 2 == 0),
                               stop=(j_ % 2 == 1), is_transpose=True),
                           waits=[("dve", vdve),
                                  ("act", ev.get(("vcons", gi - 1), 0))]
                           if j == 0 else (),
                           inc="pe")
                ev[("vtr", tg)] = p.val("pe")
                p.emit("scalar",
                       lambda e, t_=gt: e.copy(
                           sb["varena"][:, (t_ * 4) * 128:(t_ * 4 + 2) * 128],
                           sb["ps6"][:, 0:256]),
                       waits=[("pe", ev[("vtr", tg)])], inc="act")
                p.emit("scalar",
                       lambda e, t_=gt: e.copy(
                           sb["varena"][:, (t_ * 4 + 2) * 128:(t_ * 4 + 4) * 128],
                           sb["ps7"][:, 0:256]),
                       inc="act")
                ev[("vcons", gi)] = p.val("act")
        ev[("pe_last", tg)] = p.val("pe")



def phase_attn(p, io, sb):
    eb, em = sb["eb"], sb["em"]
    o_bank, l_bank = sb["ps4"], sb["ps5"]
    qld = {}
    prev_fin_dve = 0
    prev_lb_dve = 0
    prev_pair_exp = [0, 0]
    for qh in range(4):
        for qb in range(16):
            gi = qh * 16 + qb
            qbuf = gi % 2
            qdsem = "qdA" if qbuf == 0 else "qdB"
            p.emit("sync",
                   lambda e, h=qh, b=qb, q_=qbuf: e.dma_start(
                       out=sb["qd"][:, q_ * TT:(q_ + 1) * TT],
                       in_=io["q_dram"][h, :, b * TT:(b + 1) * TT]),
                   waits=[("pe", qld.get(("pe", gi - 2), 0))], inc=qdsem, by=16)
            qld[gi] = (qdsem, p.val(qdsem))
            nhb = 2 * (qb + 1)
            exp_m = {}
            lpv_m = {}
            cons = {}

            def lpv(hb, last):
                c_, cw = cons[hb]
                first = (hb == 0)
                if first:
                    cw = cw + [("dve", prev_fin_dve)]
                for kc in range(2):
                    p.emit("tensor",
                           lambda e, c=c_, k=kc, f=first: e.matmul(
                               l_bank[0:1, 0:TT], sb["ones_c"][:, :],
                               c[:, k * TT:(k + 1) * TT],
                               start=(f and k == 0), stop=(last and k == 1)),
                           waits=cw if kc == 0 else (), inc="pe")
                for kc in range(2):
                    gc = 2 * hb + kc
                    p.emit("tensor",
                           lambda e, c=c_, k=kc, g=gc, f=first: e.matmul(
                               o_bank[:, :],
                               sb["varena"][:, g * 128:(g + 1) * 128],
                               c[:, k * TT:(k + 1) * TT],
                               start=(f and k == 0), stop=(last and k == 1)),
                           inc="pe")
                lpv_m[hb] = p.val("pe")

            for hb in range(nhb):
                pair = sb["psA"].ap()[:, (hb % 2) * 1024:(hb % 2 + 1) * 1024]
                for kc in range(2):
                    w = []
                    if kc == 0:
                        w = [("act", exp_m.get(hb - 2, prev_pair_exp[hb % 2]))]
                        if hb == 0:
                            w.append(qld[gi])
                    p.emit("tensor",
                           lambda e, pr=pair, k=(2 * hb + kc) * 128, c=kc,
                           q_=qbuf: e.matmul(
                               pr[:, c * 512:(c + 1) * 512],
                               sb["karena"][:, k:k + 128],
                               sb["qd"][:, q_ * TT:(q_ + 1) * TT],
                               start=True, stop=True),
                           waits=w, inc="pe")
                s_done = p.val("pe")
                if hb > 0:
                    lpv(hb - 1, False)
                ebv = eb[:, (hb % 2) * 1024:(hb % 2 + 1) * 1024]
                p.emit("scalar",
                       lambda e, o=ebv, pr=pair: e.activation(
                           out=o, in_=pr, func=AF.Exp, scale=SCALE),
                       waits=[("pe", s_done), ("pe", lpv_m.get(hb - 2, 0))],
                       inc="act")
                exp_m[hb] = p.val("act")
                if hb // 2 == qb:
                    emv = em[:, (hb % 2) * 1024:(hb % 2 + 1) * 1024]
                    p.emit("vector",
                           lambda e, o=emv, s_=ebv, m=sb["mk"][
                               :, (hb % 2) * 1024:(hb % 2 + 1) * 1024]:
                           e.tensor_tensor(o, s_, m, A.mult),
                           waits=[("act", exp_m[hb]),
                                  ("pe", lpv_m.get(hb - 1, 0))], inc="dve")
                    cons[hb] = (emv, [("dve", p.val("dve"))])
                else:
                    cons[hb] = (ebv, [("act", exp_m[hb])])
            lpv(nhb - 1, True)
            prev_pair_exp = [exp_m[nhb - 2], exp_m[nhb - 1]]
            pv_done = p.val("pe")
            r_row = sb["rows"][0:1, 3 * TT:4 * TT]
            p.emit("vector", lambda e, r=r_row: e.reciprocal(r, l_bank[0:1, 0:TT]),
                   waits=[("pe", pv_done)], inc="dve")
            p.emit("scalar",
                   lambda e, r=r_row: e.activation(
                       out=sb["r16"][0:1, :], in_=r, func=AF.Copy,
                       bias=0.0, scale=1.0),
                   waits=[("dve", p.val("dve"))], inc="act")
            p.emit("scalar",
                   lambda e, r=r_row: e.copy(sb["rows"][0:1, 0:TT], r),
                   inc="act")
            p.emit("tensor",
                   lambda e: e.matmul(sb["ps6"][:, :], sb["ones_r16"][:, :],
                                      sb["r16"][0:1, :], start=True, stop=True),
                   waits=[("act", p.val("act")), ("dve", prev_lb_dve)], inc="pe")
            p.emit("vector",
                   lambda e: e.tensor_copy(sb["lb"][:, :], sb["ps6"][:, :]),
                   waits=[("pe", p.val("pe"))], inc="dve")
            prev_lb_dve = p.val("dve")
            p.emit("vector",
                   lambda e, b=qb: e.tensor_tensor(
                       sb["cstg"][:, (b % 2) * TT:(b % 2 + 1) * TT],
                       o_bank[:, :], sb["lb"][:, :], A.mult),
                   waits=[("ctp", qld.get(("st", gi - 1), 0))], inc="dve")
            prev_fin_dve = p.val("dve")
            qld[("pe", gi)] = p.val("pe")
            p.emit("gpsimd",
                   lambda e, h=qh, b=qb: e.dma_start(
                       out=io["ctx_part"][b // 2, h, :, (b % 2) * TT:(b % 2 + 1) * TT],
                       in_=sb["cstg"][:, (b % 2) * TT:(b % 2 + 1) * TT]),
                   waits=[("dve", prev_fin_dve)], inc="ctp", by=16)
            qld[("st", gi)] = p.val("ctp")

    for oc in range(2):
        p.emit("gpsimd",
               lambda e, b=oc, o=oc: e.dma_start(
                   out=sb["wsb"][:, b * D:(b + 1) * D],
                   in_=io["wo_s"][o].rearrange("p c l -> p (c l)")),
               waits=[("act", p.val("act"))],
               inc="wgA" if oc == 0 else "wgB", by=16)

    def do_a2a(e):
        with allow_ici_all_to_all():
            return e.collective_compute(
                "AllToAll", A.bypass, replica_groups=[list(range(NC))],
                ins=[io["ctx_part"].rearrange("a h p t -> (a h p) t")],
                outs=[io["ctx_own"].rearrange("a h p t -> (a h p) t")])

    p.emit("gpsimd", do_a2a, waits=[("ctp", p.val("ctp"))], inc="ccc")
    phase_wo(p, io, sb)



def phase_wo(p, io, sb):
    wsb = sb["wsb"]
    attn_act = p.val("act")
    attn_dve = p.val("dve")

    def ctxv(c):
        if c < 16:
            return sb["xh"][:, c * 1024:(c + 1) * 1024]
        return sb["x2"][:, (c - 16) * 1024:(c - 15) * 1024]

    for c in range(DC):
        for t2 in range(2):
            p.emit("sync",
                   lambda e, c_=c, t_=t2: e.dma_start(
                       out=ctxv(c_)[:, t_ * TT:(t_ + 1) * TT],
                       in_=io["ctx_own"][c_ // 4, c_ % 4, :,
                                         t_ * TT:(t_ + 1) * TT]),
                   waits=[("ccc", 1)], inc="cxl", by=16)
    ctx_ld = ("cxl", p.val("cxl"))
    wld = {}
    ev = {}
    st = {}
    for oc in range(DC):
        buf = oc % 2
        wgsem = "wgA" if buf == 0 else "wgB"
        if oc >= 2:
            p.emit("gpsimd",
                   lambda e, b=buf, o=oc: e.dma_start(
                       out=wsb[:, b * D:(b + 1) * D],
                       in_=io["wo_s"][o].rearrange("p c l -> p (c l)")),
                   waits=[("pe", wld.get(oc - 2, 0))], inc=wgsem, by=16)
        wl = (wgsem, p.val(wgsem))
        xsem = "xsA" if buf == 0 else "xsB"
        xbuf = sb["xs"][:, 0:2 * TT] if buf == 0 else sb["rs"][:, 0:2 * TT]
        p.emit("sync",
               lambda e, o=oc, xb=xbuf: e.dma_start(
                   out=xb,
                   in_=io["x_mid"][o * 128:(o + 1) * 128, :]),
               waits=[("dve", ev.get(oc - 2, attn_dve))], inc=xsem, by=16)
        xl = (xsem, p.val(xsem))
        for t2 in range(2):
            bank = sb[f"ps{buf * 2 + t2}"]
            for dc in range(DC):
                w = []
                if dc == 0 and t2 == 0:
                    w = [wl, ctx_ld,
                         ("dve", ev.get(oc - 2, attn_dve)),
                         ("act", attn_act)]
                p.emit("tensor",
                       lambda e, b=bank, bf=buf, d=dc, t_=t2: e.matmul(
                           b[:, :],
                           wsb[:, bf * D + d * 128:bf * D + (d + 1) * 128],
                           ctxv(d)[:, t_ * TT:(t_ + 1) * TT],
                           start=(d == 0), stop=(d == DC - 1)),
                       waits=w, inc="pe")
        wld[oc] = p.val("pe")
        obuf = sb["ostg"][:, 0:2 * TT] if buf == 0 else sb["rc"][:, 0:2 * TT]
        w = [("pe", wld[oc]), xl, st.get(oc - 2, ("stA", 0))]
        for t2 in range(2):
            bank = sb[f"ps{buf * 2 + t2}"]
            p.emit("vector",
                   lambda e, b=bank, t_=t2, ob=obuf, xb=xbuf:
                   e.scalar_tensor_tensor(
                       ob[:, t_ * TT:(t_ + 1) * TT], b[:, :], 0.0,
                       xb[:, t_ * TT:(t_ + 1) * TT], A.add, A.add),
                   waits=w if t2 == 0 else (), inc="dve")
        ev[oc] = p.val("dve")
        ssem = "stA" if buf == 0 else "stB"
        p.emit("gpsimd",
               lambda e, o=oc, ob=obuf: e.dma_start(
                   out=io["x_attn"][o * 128:(o + 1) * 128, :],
                   in_=ob),
               waits=[("dve", ev[oc])], inc=ssem, by=16)
        st[oc] = (ssem, p.val(ssem))



def _rope_tables():
    import jax
    import jax.numpy as jnp
    cpu = jax.local_devices(backend="cpu")[0]
    with jax.default_device(cpu):
        return _rope_tables_impl(jnp)


def _rope_tables_impl(jnp):
    ROPE = dict(rope_theta=500000.0, factor=32.0, hi_freq_factor=4.0,
                lo_freq_factor=1.0, original_context_length=8192)
    head_dim = DH
    pos = jnp.arange(T)
    idx = jnp.arange(0, head_dim, 2, dtype=jnp.float32) / head_dim
    freq = 1.0 / (2.0 * math.pi) * ROPE['rope_theta'] ** (-idx)
    factor, lo, hi = (ROPE['factor'], ROPE['lo_freq_factor'], ROPE['hi_freq_factor'])
    L0 = ROPE['original_context_length']
    freq_low, freq_high = (lo / L0, hi / L0)
    freq_scaled = jnp.where(freq < freq_low, freq / factor, freq)
    smooth = jnp.clip((L0 * freq - lo) / (hi - lo), 0.0, 1.0)
    freq_smooth = (1.0 - smooth) * (freq / factor) + smooth * freq
    is_mid = (freq >= freq_low) & (freq <= freq_high)
    freq = jnp.where(is_mid, freq_smooth, freq_scaled)
    phase = 2.0 * math.pi * pos.astype(jnp.float32)[:, None] * freq[None, :]
    emb = jnp.concatenate([phase, phase], axis=-1)
    cos = np.asarray(jnp.cos(emb), dtype=np.float32)
    sin = np.asarray(jnp.sin(emb), dtype=np.float32)
    cos_d = np.ascontiguousarray(cos.T)
    sinrot = np.ascontiguousarray(sin.T)
    sinrot[0:64, :] *= -1.0
    return cos_d, sinrot


def _slab(w):
    Dd, M = w.shape
    return np.ascontiguousarray(
        w.reshape(Dd // 128, 128, M // 128, 128).transpose(2, 1, 0, 3))


def kernel(L0_attn_resid, L0_attn_Wq, L0_attn_Wk, L0_attn_Wv, L0_attn_Wo,
           L0_ffn_w_up_codes, L0_ffn_w_up_absmax, L0_ffn_b_up,
           L0_ffn_w_up_lora_a, L0_ffn_w_up_lora_b, L0_ffn_w_down_codes,
           L0_ffn_w_down_absmax, L0_ffn_b_down, L0_ffn_w_down_lora_a,
           L0_ffn_w_down_lora_b):
    resid = np.ascontiguousarray(np.asarray(L0_attn_resid, dtype=np.float32)[0].T)
    wq = np.asarray(L0_attn_Wq, dtype=np.float32)
    wk = np.asarray(L0_attn_Wk, dtype=np.float32)
    wv = np.asarray(L0_attn_Wv, dtype=np.float32)
    wo = np.asarray(L0_attn_Wo, dtype=np.float32)
    upc = np.asarray(L0_ffn_w_up_codes, dtype=np.int32).reshape(H, D)
    upam = np.asarray(L0_ffn_w_up_absmax, dtype=np.float32).reshape(H, D // BLK)
    dnc = np.asarray(L0_ffn_w_down_codes, dtype=np.int32).reshape(D, H)
    dnam = np.asarray(L0_ffn_w_down_absmax, dtype=np.float32).reshape(D, H // BLK)
    bup = np.asarray(L0_ffn_b_up, dtype=np.float32)
    bdn = np.asarray(L0_ffn_b_down, dtype=np.float32)
    aup = np.asarray(L0_ffn_w_up_lora_a, dtype=np.float32)
    bu = np.asarray(L0_ffn_w_up_lora_b, dtype=np.float32)
    adn = np.asarray(L0_ffn_w_down_lora_a, dtype=np.float32)
    bd = np.asarray(L0_ffn_w_down_lora_b, dtype=np.float32)

    cos_d, sinrot = _rope_tables()
    masks = np.zeros((128, 4 * TT), dtype=ml_dtypes.bfloat16)
    for kc in range(4):
        kg = kc * 128 + np.arange(128)[:, None]
        qg = np.arange(TT)[None, :]
        masks[:, kc * TT:(kc + 1) * TT] = (kg <= qg).astype(ml_dtypes.bfloat16)
    ident = np.eye(128, dtype=np.float32)

    wo_slab = _slab(wo)
    bup_h = np.ascontiguousarray(bup.reshape(HC, 128).T)
    aupT = np.ascontiguousarray(aup.T)
    adnT = np.ascontiguousarray(adn.T)

    in_maps = []
    for c in range(NC):
        t0 = c * TOWN
        slabs = list(range(c, HC, NC))
        upc_sl = np.zeros((UPS * 128, D), dtype=np.int32)
        upam_sl = np.zeros((UPS, 128, BLK), dtype=np.float32)
        bu_sl = np.zeros((16, UPS * 128), dtype=np.float32)
        for j, g in enumerate(slabs):
            upc_sl[j * 128:(j + 1) * 128] = upc[g * 128:(g + 1) * 128]
            upam_sl[j] = upam[g * 128:(g + 1) * 128]
            bu_sl[:, j * 128:(j + 1) * 128] = bu[:, g * 128:(g + 1) * 128]
        upam_sl = np.ascontiguousarray(
            upam_sl.transpose(1, 0, 2).reshape(128, UPS * BLK))
        d0 = c * 512
        dnc_sl = np.ascontiguousarray(dnc[d0:d0 + 512])
        dnam_sl = np.ascontiguousarray(
            dnam[d0:d0 + 512].reshape(4, 128, H // BLK).transpose(1, 0, 2)
            .reshape(128, 4 * (H // BLK)))
        bd_sl = np.ascontiguousarray(bd[:, d0:d0 + 512])

        in_maps.append({
            "resid_in": np.ascontiguousarray(resid[:, t0:t0 + TOWN]),
            "wq_s": _slab(wq[:, c * 512:(c + 1) * 512]),
            "wk_s": _slab(wk[:, c * 128:(c + 1) * 128]),
            "wv_s": _slab(wv[:, c * 128:(c + 1) * 128]),
            "wo_s": wo_slab,
            "upc": upc_sl,
            "upam": upam_sl,
            "dnc": dnc_sl,
            "dnam": dnam_sl,
            "bup": bup_h,
            "bdn": np.ascontiguousarray(bdn.reshape(DC, 128).T),
            "aupT": aupT,
            "adnT": adnT,
            "bu_sl": bu_sl,
            "bd_sl": bd_sl,
            "ropec": cos_d,
            "ropes": sinrot,
            "masks": masks,
            "ident": ident,
        })

    nc = build_program()
    res = run_bass_kernel_spmd(nc, in_maps, core_ids=list(range(NC)))
    out = np.empty((1, T, D), dtype=np.float32)
    for c in range(NC):
        out[0, c * TOWN:(c + 1) * TOWN, :] = res.results[c]["out_d"].T
    return out


# version 42 (final)
# speedup vs baseline: 1.0145x; 1.0023x over previous
import contextlib
import math
import numpy as np
import ml_dtypes

import concourse.bass as bass
import concourse.mybir as mybir
from concourse.bass_utils import run_bass_kernel_spmd
from concourse.aot_env import allow_ici_all_to_all

F32 = mybir.dt.float32
BF16 = mybir.dt.bfloat16
FP16 = mybir.dt.float16
I32 = mybir.dt.int32
A = mybir.AluOpType
AF = mybir.ActivationFunctionType

T = 8192
D = 4096
H = 11008
DH = 128
NC = 8
TOWN = 1024
NTT = 2
TT = 512
DC = D // 128
HC = H // 128
BLK = 64
EPS = 1e-5
SCALE = 1.0 / math.sqrt(DH)

UPS = 11

NF4 = np.array([-1.0, -0.6961928009986877, -0.5250730514526367, -0.39491748809814453,
                -0.28444138169288635, -0.18477343022823334, -0.09105003625154495, 0.0,
                0.07958029955625534, 0.16093020141124725, 0.24611230194568634,
                0.33791524171829224, 0.44070982933044434, 0.5626170039176941,
                0.7229568362236023, 1.0], dtype=np.float64)


ENGINES = ("sync", "scalar", "vector", "tensor", "gpsimd")
ESEM = {"sync": "syn", "scalar": "act", "vector": "dve",
        "tensor": "pe", "gpsimd": "gpc"}

SYNC_STREAMS = (["cst", "xsA", "xsB", "wsA", "wsB", "rpA", "rpB", "xhl",
                 "qdA", "qdB", "cxl"] + [f"cd{k}" for k in range(9)])
GP_STREAMS = ["lra", "stg", "wgA", "wgB", "stA", "stB",
              "qsA", "qsB", "xps", "ctp", "bdc"]
SEM_NAMES = (["pe", "act", "dve", "gpc", "syn",
              "ccu", "ccd", "ccx", "ccc"] + SYNC_STREAMS + GP_STREAMS)


class P:

    def __init__(self, nc, sems, cur):
        self.nc = nc
        self.sems = sems
        self.cur = cur
        self.cnt = {s: 0 for s in sems}
        self.waited = {}
        self.eng = {n: getattr(nc, n) for n in ENGINES}

    def val(self, sem):
        return self.cnt[sem]

    def emit(self, engine, build=None, waits=(), inc=None, by=1):
        for s, v in waits:
            if v <= 0:
                continue
            key = (engine, s)
            if self.waited.get(key, 0) < v:
                self.waited[key] = v
                if engine == self.cur:
                    self.eng[engine].wait_ge(self.sems[s], v)
        r = None
        if build is not None and engine == self.cur:
            r = build(self.eng[engine])
        if inc is not None:
            self.cnt[inc] += by
            if build is not None and engine == self.cur:
                r.then_inc(self.sems[inc], by)
        return r

    def barrier(self):
        self.emit("sync", lambda e: e.nop(),
                  waits=[(s, self.val(s)) for s in SYNC_STREAMS])
        self.emit("gpsimd", lambda e: e.nop(),
                  waits=[(s, self.val(s)) for s in GP_STREAMS])
        marks = {}
        for e in ENGINES:
            self.emit(e, lambda en: en.nop(), inc=ESEM[e])
            marks[e] = self.val(ESEM[e])
        for e in ENGINES:
            ws = [(ESEM[o], marks[o]) for o in ENGINES if o != e]
            self.emit(e, lambda en: en.nop(), waits=ws)



def build_program():
    import os
    rd = os.environ.get("K_RACEDET", "0") == "1"
    nc = bass.Bass("TRN2", target_bir_lowering=False, detect_race_conditions=rd)
    io = {}

    def ein(name, shape, dt):
        io[name] = nc.dram_tensor(name, shape, dt, kind="ExternalInput").ap()

    def eout(name, shape, dt):
        io[name] = nc.dram_tensor(name, shape, dt, kind="ExternalOutput").ap()

    def ind(name, shape, dt):
        io[name] = nc.dram_tensor(name, shape, dt).ap()

    ein("resid_in", [D, TOWN], F32)
    ein("wq_s", [4, 128, DC, 128], F32)
    ein("wk_s", [1, 128, DC, 128], F32)
    ein("wv_s", [1, 128, DC, 128], F32)
    ein("wo_s", [DC, 128, DC, 128], F32)
    ein("upc", [UPS * 128, D], I32)
    ein("upam", [128, UPS * BLK], F32)
    ein("dnc", [512, H], I32)
    ein("dnam", [128, 4 * (H // BLK)], F32)
    ein("bup", [128, HC], F32)
    ein("bdn", [128, DC], F32)
    ein("aupT", [16, D], F32)
    ein("adnT", [16, H], F32)
    ein("bu_sl", [16, UPS * 128], F32)
    ein("bd_sl", [16, 512], F32)
    ein("ropec", [128, T], F32)
    ein("ropes", [128, T], F32)
    ein("masks", [128, 4 * TT], BF16)
    ein("ident", [128, 128], F32)

    eout("out_d", [D, TOWN], F32)

    ind("x_mid", [D, TOWN], F32)
    ind("x_attn", [D, TOWN], F32)
    ind("wup_part", [UPS, 128, DC, 128], BF16)
    io["wup_all"] = nc.dram_tensor("wup_all", [NC * UPS, 128, DC, 128], BF16,
                                   addr_space="Shared").ap()
    ind("wdn_part", [2, 4, 128, 43, 128], BF16)
    io["wdn_all"] = nc.dram_tensor("wdn_all", [2, 4, NC, 128, 43, 128], BF16,
                                   addr_space="Shared").ap()
    ind("xh_part", [NTT, 128, DC, TT], BF16)
    io["xh_all"] = nc.dram_tensor("xh_all", [NC * NTT, 128, DC, TT], BF16,
                                   addr_space="Shared").ap()
    ind("ctx_part", [NC, 4, 128, TOWN], BF16)
    ind("ctx_own", [NC, 4, 128, TOWN], BF16)
    ind("q_dram", [4, 128, T], BF16)
    ind("dbg1", [128, DC * TT], BF16)
    ind("dbg2", [128, 43 * TT], BF16)

    stack = contextlib.ExitStack()
    sb = {}

    def sbuf(name, shape, dt):
        sb[name] = stack.enter_context(nc.sbuf_tensor(name, shape, dt))

    def psum(name, shape, dt):
        sb[name] = stack.enter_context(nc.psum_tensor(name, shape, dt))

    with stack:
        sbuf("xh", [128, DC * TT], BF16)
        sbuf("x2", [128, 43 * TT], BF16)
        sbuf("karena", [128, T], BF16)
        sbuf("varena", [128, T], BF16)
        sbuf("wsb", [128, 11008], BF16)
        sbuf("dq_w", [128, 2 * 1024], I32)
        sbuf("dq_stg", [128, 1024], BF16)
        sbuf("dq_amu", [128, 704], F32)
        sbuf("dq_amd", [128, 688], F32)
        sbuf("dq_bu", [16, 128], BF16)
        sbuf("dq_at", [16, 1024], BF16)
        sbuf("rc", [128, 2 * TT], F32)
        sbuf("rs", [128, 2 * TT], F32)
        sbuf("mk", [128, 4 * TT], BF16)
        sbuf("idn", [128, 128], F32)
        sbuf("bup_sb", [128, HC], F32)
        sbuf("bdn_sb", [128, DC], F32)
        sbuf("bdnh", [128, DC], F32)
        sbuf("ones_c", [128, 1], BF16)
        sbuf("ones_rf", [1, 128], F32)
        sbuf("ones_r16", [1, 128], FP16)
        sbuf("r16", [1, 512], FP16)
        sbuf("xs", [128, 2 * TT], F32)
        sbuf("xbf", [128, TT], BF16)
        sbuf("xsq", [128, TT], BF16)
        sbuf("rows", [1, 4 * TT], F32)
        sbuf("mb", [128, TT], F32)
        sbuf("rb_", [128, TT], F32)
        sbuf("va", [128, TT], F32)
        sbuf("vb", [128, TT], F32)
        sbuf("eb", [128, 2 * 2048], BF16)
        sbuf("em", [128, 2048], BF16)
        sb["dq_ct"] = sb["dq_w"].ap()
        sb["dq_stt"] = bass.AP(sb["karena"].ap().tensor.bitcast(F32), 0,
                               [[4096, 128], [1, 1024]])
        kv_i32 = sb["karena"].ap().tensor.bitcast(I32)
        va_i32 = sb["varena"].ap().tensor.bitcast(I32)
        sb["dq_ring"] = (
            [sb["dq_w"].ap()[:, k * 1024:(k + 1) * 1024] for k in range(2)]
            + [bass.AP(kv_i32, (k + 1) * 1024, [[4096, 128], [1, 1024]])
               for k in range(3)]
            + [bass.AP(va_i32, k * 1024, [[4096, 128], [1, 1024]])
               for k in range(4)])
        sb["dq_cf"] = bass.AP(sb["eb"].ap().tensor.bitcast(FP16), 0,
                              [[4096, 128], [1, 2048]])
        sb["dq_acc"] = bass.AP(sb["em"].ap().tensor.bitcast(FP16), 0,
                               [[2048, 128], [1, 1024]])
        sb["dq_trm"] = bass.AP(sb["em"].ap().tensor.bitcast(FP16), 1024,
                               [[2048, 128], [1, 1024]])
        sbuf("lb", [128, TT], F32)
        sbuf("qd", [128, 2 * TT], BF16)
        sbuf("cstg", [128, TOWN], BF16)
        sbuf("ostg", [128, 2 * TT], F32)
        psum("psA", [128, 2048], F32)
        psum("ps4", [128, 512], F32)
        psum("ps5", [128, 512], F32)
        psum("ps6", [128, 512], F32)
        psum("ps7", [128, 512], F32)
        for i in range(4):
            sb[f"ps{i}"] = sb["psA"].ap()[:, i * 512:(i + 1) * 512]

        semstack = contextlib.ExitStack()
        sems = {}
        with semstack, nc.Block() as block:
            for name in SEM_NAMES:
                sems[name] = semstack.enter_context(nc.semaphore("s_" + name))

            def build_all(cur):
                p = P(nc, sems, cur)
                emit_program(p, io, sb)

            @block.sync
            def _(eng):
                build_all("sync")

            @block.scalar
            def _(eng):
                build_all("scalar")

            @block.vector
            def _(eng):
                build_all("vector")

            @block.tensor
            def _(eng):
                build_all("tensor")

            @block.gpsimd
            def _(eng):
                build_all("gpsimd")

    return nc


def phase_ffn1(p, io, sb):
    DQ_COMMON["side"] = None
    DQ_COMMON["a_idx"] = 0
    DQ_COMMON["b_idx"] = 0
    DQ_COMMON["cast_marks"] = {}
    st_up = dq_state()
    st_dn = dq_state()

    def dn_tiles(hh, dcl):
        return [(dcl, hh, pc) for pc in range(6)]

    def dn_half(hh):
        for dcl in range(4):
            dq_load(p, io, sb, "dn", dn_tiles(hh, dcl))
            dq_run(p, io, sb, st_dn, "dn", dn_tiles(hh, dcl))
            dq_dn_gather(p, io, sb, hh, dcl, p.val("stg"))

    def hook(tt, half, stage, i):
        if tt != 0:
            return
        def up_tiles(s0, s1):
            return [(s, q) for s in range(s0, s1) for q in range(4)]

        if (half, stage) == (0, "up"):
            dq_load(p, io, sb, "up", up_tiles(0, 2))
            dq_up_slabs(p, io, sb, st_up, 0, 1)
            dq_load(p, io, sb, "up", up_tiles(2, 3))
            dq_up_slabs(p, io, sb, st_up, 1, 2)
        elif stage == "up_hc":
            hc = half * 43 + i
            if hc % 8 == 0 and hc // 8 in st_up.get("stg_r", {}):
                dq_up_gather(p, io, sb, st_up, hc // 8)
            if half == 0 and i in (8, 16, 24, 32):
                s = i // 8 + 1
                if s + 1 <= 5:
                    dq_load(p, io, sb, "up", up_tiles(s + 1, s + 2))
                dq_up_slabs(p, io, sb, st_up, s, s + 1)
        elif (half, stage) == (0, "down"):
            dn_half(0)
        elif (half, stage) == (0, "down_dc") and i in (5, 11, 17, 23, 29):
            s = 6 + (5, 11, 17, 23, 29).index(i)
            if s == 6:
                dq_load(p, io, sb, "up", up_tiles(6, 8))
            elif s + 1 <= 10:
                dq_load(p, io, sb, "up", up_tiles(s + 1, s + 2))
            dq_up_slabs(p, io, sb, st_up, s, s + 1)
            dq_up_gather(p, io, sb, st_up, s)
        elif (half, stage) == (1, "down"):
            dn_half(1)

    phase_ffn(p, io, sb, src="resid_in", dst="x_mid", hook=hook)


def emit_program(p, io, sb):
    import os
    np_ = int(os.environ.get("K_PHASES", "99"))
    phases = [
        lambda: phase_consts(p, io, sb),
        lambda: phase_ffn1(p, io, sb),
        lambda: phase_ln2qkv(p, io, sb),
        lambda: phase_attn(p, io, sb),
        lambda: phase_ffn(p, io, sb, src="x_attn", dst="out_d"),
    ]
    np_ += 1
    for i, ph in enumerate(phases):
        if i >= np_:
            break
        ph()
        p.barrier()
    dump = os.environ.get("K_DUMP", "")
    if dump:
        if ":" in dump:
            name, off = dump.split(":")
            off = int(off)
            nel = D * TOWN * 2
            tsz = 1
            for s_ in io[name].tensor.shape:
                tsz *= s_
            bpe = {BF16: 2, F32: 4, I32: 4}[io[name].tensor.dtype]
            nel = min(nel, tsz * bpe // 2)
            flat = bass.AP(io[name].tensor.bitcast(BF16), off, [[1, nel]])
            dstv = bass.AP(io["out_d"].tensor.bitcast(BF16), 0, [[1, nel]])
            p.emit("gpsimd", lambda e, s_=flat, d_=dstv: e.dma_start(out=d_, in_=s_),
                   inc="stA", by=16)
        else:
            p.emit("gpsimd", lambda e: e.dma_start(out=io["out_d"][:, :],
                                                   in_=io[dump][:, :]),
                   inc="stA", by=16)
    p.emit("gpsimd", lambda e: e.nop(),
           waits=[(s, p.val(s)) for s in GP_STREAMS])


def pe_fence(p, sb):
    p.emit("tensor",
           lambda e: e.matmul(sb["ps5"][0:1, 0:1], sb["ones_c"][:, :],
                              sb["ones_c"][:, :], start=True, stop=True),
           inc="pe")



def phase_consts(p, io, sb):
    p.emit("sync", lambda e: e.dma_start(out=sb["idn"][:, :], in_=io["ident"][:, :]),
           inc="cst", by=16)
    p.emit("sync", lambda e: e.dma_start(out=sb["mk"][:, :], in_=io["masks"][:, :]),
           inc="cst", by=16)
    p.emit("sync", lambda e: e.dma_start(out=sb["bup_sb"][:, :], in_=io["bup"][:, :]),
           inc="cst", by=16)
    p.emit("sync", lambda e: e.dma_start(out=sb["bdn_sb"][:, :], in_=io["bdn"][:, :]),
           inc="cst", by=16)
    p.emit("sync", lambda e: e.dma_start(out=sb["dq_amu"][:, :], in_=io["upam"][:, 0:704]),
           inc="cst", by=16)
    p.emit("sync", lambda e: e.dma_start(out=sb["dq_amd"][:, :], in_=io["dnam"][:, :]),
           inc="cst", by=16)
    p.emit("vector", lambda e: e.memset(sb["ones_c"][:, :], 1.0), inc="dve")
    p.emit("vector", lambda e: e.memset(sb["ones_rf"][:, :], 1.0), inc="dve")
    p.emit("vector", lambda e: e.memset(sb["ones_r16"][:, :], 1.0), inc="dve")
    p.emit("vector",
           lambda e: e.tensor_scalar(sb["bdnh"][:, :], sb["bdn_sb"][:, :], 0.5,
                                     None, A.mult),
           waits=[("cst", p.val("cst"))], inc="dve")



def dq_state():
    return dict(act_mark={}, dve_mark={}, stg_free={}, pe_mark={},
                lora_ld={}, i=0)


DQ_COMMON = {"side": None}
DQ_RING = 9


def dq_src_ap(io, side, tile):
    codes = io["upc"] if side == "up" else io["dnc"]
    if side == "up":
        s, q = tile
        return codes[s * 128:(s + 1) * 128, q * 1024:(q + 1) * 1024], 1024
    dcl, hh, pc = tile
    F = 1024 if pc < 5 else 384
    col0 = hh * 5504 + pc * 1024
    return codes[dcl * 128:(dcl + 1) * 128, col0:col0 + F], F


def dq_load(p, io, sb, side, tiles):
    for tile in tiles:
        idx = DQ_COMMON["a_idx"]
        DQ_COMMON["a_idx"] += 1
        assert idx - DQ_COMMON["b_idx"] < DQ_RING, (
            "stage-A code load would overwrite a ring slot whose tile has "
            "not been emitted for compute yet")
        slot = idx % DQ_RING
        src, F = dq_src_ap(io, side, tile)
        p.emit("sync",
               lambda e, o=sb["dq_ring"][slot][:, 0:F], s_=src:
               e.dma_start(out=o, in_=s_),
               waits=[("dve", DQ_COMMON["cast_marks"].get(idx - DQ_RING, 0))],
               inc=f"cd{slot}", by=16)


def dq_run(p, io, sb, st, side, tiles):
    nf4 = NF4
    ct, cf = sb["dq_ct"], sb["dq_cf"]
    codes = io["upc"] if side == "up" else io["dnc"]
    act_mark = st["act_mark"]
    dve_mark = st["dve_mark"]
    stg_free = st["stg_free"]
    pe_mark = st["pe_mark"]
    lora_ld = st["lora_ld"]
    amtab = sb["dq_amu"] if side == "up" else sb["dq_amd"]
    if DQ_COMMON["side"] != side:
        DQ_COMMON["side"] = side
        ent = {"act": p.val("act"), "pe": p.val("pe"),
               "dve": p.val("dve"), "stg": p.val("stg")}
    else:
        ent = {"act": 0, "pe": 0, "dve": 0, "stg": 0}
    kk = 0
    first_tile = True

    for tile in tiles:
        i = st["i"]
        st["i"] += 1
        if side == "up":
            s, q = tile
            F = 1024
            col0 = None
        else:
            dcl, hh, pc = tile
            F = 1024 if pc < 5 else 384
            col0 = hh * 5504 + pc * 1024
        nj = F // 128
        if True:
            for hi, h2 in enumerate(range(0, F, 1024)):
                fh = min(1024, F - h2)
                sidx = i + hi
                cbuf = sidx % 2
                idx = DQ_COMMON["b_idx"]
                DQ_COMMON["b_idx"] += 1
                slot = idx % DQ_RING
                p.emit("vector",
                       lambda e, o=cf[:, cbuf * 1024:cbuf * 1024 + fh],
                       s_=sb["dq_ring"][slot][:, 0:fh]:
                       e.tensor_copy(o, s_),
                       waits=[(f"cd{slot}", 16 * (idx // DQ_RING + 1))],
                       inc="dve")
                DQ_COMMON["cast_marks"][idx] = p.val("dve")
                cfv = cf[:, cbuf * 1024:cbuf * 1024 + fh]
                accv = sb["dq_acc"][:, 0:fh]
                trmv = sb["dq_trm"][:, 0:fh]
                p.emit("vector",
                       lambda e, o=accv, s_=cfv: e.tensor_scalar(
                           o, s_, 0.0, float(nf4[0]), A.is_equal, A.mult),
                       inc="dve")
                for v in range(1, 16):
                    p.emit("vector",
                           lambda e, o=trmv, s_=cfv, vv=v: e.tensor_scalar(
                               o, s_, float(vv), float(nf4[vv]), A.is_equal, A.mult),
                           inc="dve")
                    p.emit("vector",
                           lambda e, o=accv, t_=trmv: e.tensor_tensor(o, o, t_, A.add),
                           inc="dve")
                dve_mark[("acc", sidx)] = p.val("dve")
                nb = fh // BLK
                if side == "up":
                    off = s * BLK + (q * 1024 + h2) // BLK
                else:
                    off = dcl * (H // BLK) + (col0 + h2) // BLK
                amv = amtab[:, off:off + nb]
                am_b = amv.rearrange("p (b one) -> p b one", one=1).broadcast_to(
                    [128, nb, BLK])
                accb = accv.rearrange("p (b r) -> p b r", r=BLK)
                sttv = sb["dq_stt"][:, h2:h2 + fh].rearrange("p (b r) -> p b r", r=BLK)
                p.emit("vector",
                       lambda e, o=sttv, s_=accb, m=am_b: e.scalar_tensor_tensor(
                           o, s_, 0.0, m, A.add, A.mult),
                       waits=[("pe", max(pe_mark.get(("tr", i - 1), 0),
                                         ent["pe"] if kk < 2 else 0))],
                       inc="dve")
                dve_mark[("stt", sidx)] = p.val("dve")
                kk += 1
            stt_done = p.val("dve")

            if side == "up":
                at_src = io["aupT"][:, q * 1024:q * 1024 + F]
                b_src = io["bu_sl"][:, s * 128:(s + 1) * 128]
            else:
                at_src = io["adnT"][:, col0:col0 + F]
                b_src = io["bd_sl"][:, dcl * 128:(dcl + 1) * 128]
            p.emit("gpsimd",
                   lambda e, s_=at_src, f=F: e.dma_start(out=sb["dq_at"][:, 0:f], in_=s_),
                   waits=[("pe", max(pe_mark.get(("tr", i - 1), 0),
                                     ent["pe"] if first_tile else 0))],
                   inc="lra", by=16)
            p.emit("gpsimd",
                   lambda e, s_=b_src: e.dma_start(out=sb["dq_bu"][:, :], in_=s_),
                   inc="lra", by=16)
            lora_ld[i] = p.val("lra")

            wv = [("dve", stt_done), ("lra", lora_ld[i])]
            for j in range(nj):
                bank = sb["ps6"] if j < 4 else sb["ps7"]
                col = (j % 4) * 128
                bfirst = (j % 4 == 0)
                blast = (j == nj - 1) or (j % 4 == 3)
                p.emit("tensor",
                       lambda e, b_=bank, c=col, j_=j, f=bfirst: e.matmul(
                           b_[:, c:c + 128], sb["dq_stt"][:, j_ * 128:(j_ + 1) * 128],
                           sb["idn"][:, :], start=f, stop=False, is_transpose=True),
                       waits=wv if j == 0 else (), inc="pe")
                p.emit("tensor",
                       lambda e, b_=bank, c=col, j_=j, l_=blast: e.matmul(
                           b_[:, c:c + 128], sb["dq_at"][:, j_ * 128:(j_ + 1) * 128],
                           sb["dq_bu"][:, :], start=False, stop=l_),
                       inc="pe")
            pe_mark[("tr", i)] = p.val("pe")

            sgb = 0
            p.emit("vector",
                   lambda e, o=sb["dq_stg"][:, sgb:sgb + min(nj, 4) * 128]:
                   e.tensor_copy(o, sb["ps6"][:, 0:min(nj, 4) * 128]),
                   waits=[("pe", pe_mark[("tr", i)]),
                          ("stg", max(stg_free.get(i - 1, 0),
                                      ent["stg"] if first_tile else 0))],
                   inc="dve")
            if nj > 4:
                p.emit("vector",
                       lambda e, o=sb["dq_stg"][:, sgb + 512:sgb + 512 + (nj - 4) * 128]:
                       e.tensor_copy(o, sb["ps7"][:, 0:(nj - 4) * 128]),
                       inc="dve")
            act_mark[("evac", i)] = p.val("dve")

            if side == "up":
                dst = io["wup_part"][s, :, 8 * q:8 * q + nj, :].rearrange(
                    "p c l -> p (c l)")
            else:
                dst = io["wdn_part"][hh, dcl, :, 8 * pc:8 * pc + nj, :].rearrange(
                    "p c l -> p (c l)")
            p.emit("gpsimd",
                   lambda e, d=dst, o=sb["dq_stg"][:, sgb:sgb + nj * 128]:
                   e.dma_start(out=d, in_=o),
                   waits=[("dve", act_mark[("evac", i)])], inc="stg", by=16)
            stg_free[i] = p.val("stg")
            first_tile = False


def dq_up_slabs(p, io, sb, st, s0, s1):
    for s in range(s0, s1):
        dq_run(p, io, sb, st, "up", [(s, q) for q in range(4)])
        st.setdefault("stg_r", {})[s] = p.val("stg")


def dq_up_gather(p, io, sb, st, r):
    if r in st.setdefault("g_done", set()):
        return
    st["g_done"].add(r)
    p.emit("gpsimd",
           lambda e, s_=r: e.collective_compute(
               "AllGather", A.bypass, replica_groups=[list(range(NC))],
               ins=[io["wup_part"][s_].rearrange("p c l -> p (c l)")],
               outs=[io["wup_all"][8 * s_:8 * s_ + 8]
                     .rearrange("s p c l -> (s p) (c l)")]),
           waits=[("stg", st["stg_r"][r])], inc="ccu")


def dq_dn_gather(p, io, sb, hh, dcl, stg_mark):
    p.emit("gpsimd",
           lambda e, h=hh, d=dcl: e.collective_compute(
               "AllGather", A.bypass, replica_groups=[list(range(NC))],
               ins=[io["wdn_part"][h, d].rearrange("p c l -> p (c l)")],
               outs=[io["wdn_all"][h, d].rearrange("s p c l -> (s p) (c l)")]),
           waits=[("stg", stg_mark)], inc="ccd")



def snap(p):
    return {s: p.val(s) for s in ("pe", "act", "dve", "stA", "stB")}


def emit_ln(p, io, sb, src_ap, tt, lnpart=9, carry=None):
    if carry is None:
        carry = snap(p)
    ps_sum, ps_sq = sb["ps4"], sb["ps5"]
    xs = sb["xs"]
    ld = {}
    cast_m = {}
    pe_m = {}
    for dc in range(DC):
        buf = dc % 2
        xsem = "xsA" if buf == 0 else "xsB"
        p.emit("sync",
               lambda e, b=buf, d=dc: e.dma_start(
                   out=xs[:, b * TT:(b + 1) * TT],
                   in_=src_ap[d * 128:(d + 1) * 128, tt * TT:(tt + 1) * TT]),
               waits=[("act", cast_m.get(dc - 2, 0)),
                      ("dve", carry["dve"] if dc < 2 else 0)],
               inc=xsem, by=16)
        ld[dc] = (xsem, p.val(xsem))
        p.emit("scalar",
               lambda e, b=buf: e.copy(sb["xbf"][:, :], xs[:, b * TT:(b + 1) * TT]),
               waits=[ld[dc], ("pe", pe_m.get(dc - 1, carry["pe"] if dc < 1 else 0))],
               inc="act")
        p.emit("scalar",
               lambda e, b=buf: e.activation(
                   out=sb["xsq"][:, :], in_=xs[:, b * TT:(b + 1) * TT],
                   func=AF.Square),
               inc="act")
        cast_m[dc] = p.val("act")
        p.emit("tensor",
               lambda e, d=dc: e.matmul(ps_sum[0:1, :], sb["ones_c"][:, :],
                                        sb["xbf"][:, :], start=(d == 0),
                                        stop=(d == DC - 1)),
               waits=[("act", cast_m[dc])], inc="pe")
        p.emit("tensor",
               lambda e, d=dc: e.matmul(ps_sq[0:1, :], sb["ones_c"][:, :],
                                        sb["xsq"][:, :], start=(d == 0),
                                        stop=(d == DC - 1)),
               inc="pe")
        pe_m[dc] = p.val("pe")
    stats_pe = p.val("pe")
    stats_act = p.val("act")
    if lnpart <= 1:
        return p.val("dve")
    rows = sb["rows"]
    m_row = rows[0:1, 0 * TT:1 * TT]
    s_row = rows[0:1, 1 * TT:2 * TT]
    v_row = rows[0:1, 2 * TT:3 * TT]
    r_row = rows[0:1, 3 * TT:4 * TT]
    p.emit("scalar",
           lambda e: e.activation(out=m_row, in_=ps_sum[0:1, :], func=AF.Copy,
                                  bias=0.0, scale=1.0 / D),
           waits=[("pe", stats_pe)], inc="act")
    p.emit("scalar",
           lambda e: e.activation(out=s_row, in_=ps_sq[0:1, :], func=AF.Copy,
                                  bias=0.0, scale=1.0 / D),
           inc="act")
    p.emit("vector", lambda e: e.tensor_tensor(v_row, m_row, m_row, A.mult),
           waits=[("act", p.val("act"))], inc="dve")
    p.emit("vector", lambda e: e.tensor_tensor(v_row, s_row, v_row, A.subtract),
           inc="dve")
    p.emit("vector",
           lambda e: e.tensor_scalar(v_row, v_row, EPS, None, A.add),
           inc="dve")
    p.emit("scalar",
           lambda e: e.activation(out=s_row, in_=v_row, func=AF.Sqrt),
           waits=[("dve", p.val("dve"))], inc="act")
    d_row = rows[0:1, 2 * TT:3 * TT]
    p.emit("scalar",
           lambda e, d=d_row: e.copy(d, s_row),
           inc="act")
    p.emit("vector", lambda e: e.reciprocal(r_row, s_row),
           waits=[("act", p.val("act"))], inc="dve")
    if lnpart <= 2:
        return p.val("dve")
    p.emit("tensor",
           lambda e: e.matmul(sb["ps6"][:, :], sb["ones_rf"][:, :], m_row,
                              start=True, stop=True),
           waits=[("dve", p.val("dve")), ("act", carry["act"])], inc="pe")
    p.emit("tensor",
           lambda e: e.matmul(sb["ps7"][:, :], sb["ones_rf"][:, :], r_row,
                              start=True, stop=True),
           inc="pe")
    bc_pe = p.val("pe")
    pe_fence(p, sb)
    if lnpart <= 3:
        return p.val("dve")
    p.emit("vector", lambda e: e.tensor_copy(sb["mb"][:, :], sb["ps6"][:, :]),
           waits=[("pe", p.val("pe")), ("dve", p.val("dve"))], inc="dve")
    p.emit("vector", lambda e: e.tensor_copy(sb["rb_"][:, :], sb["ps7"][:, :]),
           inc="dve")
    bc_dve = p.val("dve")
    if lnpart <= 4:
        return p.val("dve")
    ap_m = {}
    for dc in range(DC):
        buf = dc % 2
        xsem = "xsA" if buf == 0 else "xsB"
        w = [("dve", ap_m.get(dc - 2, 0))]
        if dc < 2:
            w.append(("act", stats_act))
        p.emit("sync",
               lambda e, b=buf, d=dc: e.dma_start(
                   out=xs[:, b * TT:(b + 1) * TT],
                   in_=src_ap[d * 128:(d + 1) * 128, tt * TT:(tt + 1) * TT]),
               waits=w, inc=xsem, by=16)
        mark = (xsem, p.val(xsem))
        p.emit("vector",
               lambda e, b=buf: e.tensor_tensor(sb["va"][:, :],
                                                xs[:, b * TT:(b + 1) * TT],
                                                sb["mb"][:, :], A.subtract),
               waits=[mark, ("dve", bc_dve)], inc="dve")
        p.emit("vector",
               lambda e, d=dc: e.tensor_tensor(
                   sb["xh"][:, d * TT:(d + 1) * TT], sb["va"][:, :],
                   sb["rb_"][:, :], A.mult),
               inc="dve")
        ap_m[dc] = p.val("dve")
    return p.val("dve")



def phase_ffn(p, io, sb, src, dst, hook=None):
    final = (dst == "out_d")
    src_ap = io[src]
    inter = io["x_mid"]
    wsb = sb["wsb"]
    for tt in range(NTT):
        ln_done = emit_ln(p, io, sb, src_ap, tt)
        for half in range(2):
            if hook:
                hook(tt, half, "up", 0)
            cu = snap(p)
            relu_m = {}
            up_pe = {}
            slab_ld = {}
            for hcl in range(43):
                hc = half * 43 + hcl
                if hook:
                    hook(tt, half, "up_hc", hcl)
                buf = hcl % 2
                wsem = "wsA" if buf == 0 else "wsB"
                p.emit("sync",
                       lambda e, b=buf, s=hc: e.dma_start(
                           out=wsb[:, b * D:(b + 1) * D],
                           in_=io["wup_all"][s].rearrange("p c l -> p (c l)")),
                       waits=[("pe", up_pe.get(hcl - 2, cu["pe"])),
                              ("ccu", hc // 8 + 1)],
                       inc=wsem, by=16)
                slab_ld[hcl] = (wsem, p.val(wsem))
                bank = sb[f"ps{hcl % 2}"]
                for dc in range(DC):
                    w = []
                    if dc == 0:
                        w = [slab_ld[hcl],
                             ("act", relu_m.get(hcl - 2, cu["act"])),
                             ("dve", ln_done)]
                    p.emit("tensor",
                           lambda e, b=bank, bf=buf, d=dc: e.matmul(
                               b[:, :],
                               wsb[:, bf * D + d * 128:bf * D + (d + 1) * 128],
                               sb["xh"][:, d * TT:(d + 1) * TT],
                               start=(d == 0), stop=(d == DC - 1)),
                           waits=w, inc="pe")
                up_pe[hcl] = p.val("pe")
                p.emit("scalar",
                       lambda e, b=bank, h=hc, hl=hcl: e.activation(
                           out=sb["x2"][:, hl * TT:(hl + 1) * TT], in_=b[:, :],
                           func=AF.Relu, bias=sb["bup_sb"][:, h:h + 1], scale=1.0),
                       waits=[("pe", up_pe[hcl])], inc="act")
                relu_m[hcl] = p.val("act")
            up_done_act = p.val("act")
            up_pe_last = p.val("pe")
            if hook:
                hook(tt, half, "down", 0)
            cd = snap(p)
            dn_ld = {}
            dn_pe = {}
            dn_ev = {}
            dn_st = {}
            rd = src_ap if half == 0 else inter
            wr = inter if half == 0 else io[dst]
            for di, dc in enumerate(dcl + 4 * k
                                    for dcl in range(4) for k in range(8)):
                if hook:
                    hook(tt, half, "down_dc", di)
                buf = di % 2
                wsem = "wsA" if buf == 0 else "wsB"
                xsem = "xsA" if buf == 0 else "xsB"
                ccd_w = ("ccd", half * 4 + dc % 4 + 1)
                wv = [("pe", dn_pe.get(di - 2, 0)), ccd_w]
                if di < 2:
                    wv = [("pe", up_pe_last), ccd_w]
                p.emit("sync",
                       lambda e, b=buf, d=dc, h=half: e.dma_start(
                           out=wsb[:, b * 5504:(b + 1) * 5504],
                           in_=io["wdn_all"][h, d % 4, d // 4]
                           .rearrange("p c l -> p (c l)")),
                       waits=wv, inc=wsem, by=16)
                dn_ld[di] = (wsem, p.val(wsem))
                wv = [("dve", dn_ev.get(di - 2, max(ln_done, cd["dve"])
                                        if di < 2 else 0))]
                if half == 1 and di < 2:
                    wv += [("stA", cd["stA"]), ("stB", cd["stB"])]
                p.emit("sync",
                       lambda e, b=buf, d=dc, r=rd: e.dma_start(
                           out=sb["xs"][:, b * TT:(b + 1) * TT],
                           in_=r[d * 128:(d + 1) * 128, tt * TT:(tt + 1) * TT]),
                       waits=wv, inc=xsem, by=16)
                xs_ld = (xsem, p.val(xsem))
                bank = sb[f"ps{2 + di % 2}"]
                for hcl in range(43):
                    w = []
                    if hcl == 0:
                        w = [dn_ld[di], ("act", up_done_act),
                             ("dve", dn_ev.get(di - 2, cd["dve"]))]
                    last = (hcl == 42)
                    p.emit("tensor",
                           lambda e, b=bank, bf=buf, hl=hcl, l_=last: e.matmul(
                               b[:, :],
                               wsb[:, bf * 5504 + hl * 128:bf * 5504 + (hl + 1) * 128],
                               sb["x2"][:, hl * TT:(hl + 1) * TT],
                               start=(hl == 0), stop=l_),
                           waits=w, inc="pe")
                dn_pe[di] = p.val("pe")
                ob = di % 2
                wprev = dn_st.get(di - 2,
                                  ("stA" if ob == 0 else "stB",
                                   cd["stA" if ob == 0 else "stB"]))
                w = [("pe", dn_pe[di]), xs_ld, wprev]
                if half == 1:
                    p.emit("vector",
                           lambda e, b=bank, d=dc: e.tensor_scalar(
                               sb["va"][:, :], b[:, :], 0.5,
                               sb["bdnh"][:, d:d + 1], A.mult, A.add),
                           waits=w, inc="dve")
                    p.emit("vector",
                           lambda e, bf=buf, o=ob: e.tensor_tensor(
                               sb["ostg"][:, o * TT:(o + 1) * TT], sb["va"][:, :],
                               sb["xs"][:, bf * TT:(bf + 1) * TT], A.add),
                           inc="dve")
                else:
                    p.emit("vector",
                           lambda e, b=bank, bf=buf, o=ob: e.scalar_tensor_tensor(
                               sb["ostg"][:, o * TT:(o + 1) * TT], b[:, :], 0.5,
                               sb["xs"][:, bf * TT:(bf + 1) * TT], A.mult, A.add),
                           waits=w, inc="dve")
                dn_ev[di] = p.val("dve")
                ssem = "stA" if ob == 0 else "stB"
                p.emit("gpsimd",
                       lambda e, d=dc, o=ob, w_=wr: e.dma_start(
                           out=w_[d * 128:(d + 1) * 128, tt * TT:(tt + 1) * TT],
                           in_=sb["ostg"][:, o * TT:(o + 1) * TT]),
                       waits=[("dve", dn_ev[di])], inc=ssem, by=16)
                dn_st[di] = (ssem, p.val(ssem))



def phase_ln2qkv(p, io, sb):
    wsb = sb["wsb"]
    for oc in range(6):
        if oc < 4:
            wsrc, dst = io["wq_s"][oc], sb["x2"][:, oc * D:(oc + 1) * D]
        elif oc == 4:
            wsrc, dst = io["wk_s"][0], wsb[:, 0:D]
        else:
            wsrc, dst = io["wv_s"][0], wsb[:, D:2 * D]
        p.emit("gpsimd",
               lambda e, s_=wsrc, d_=dst: e.dma_start(
                   out=d_, in_=s_.rearrange("p c l -> p (c l)")),
               inc="wgA", by=16)
    slab_ld = ("wgA", p.val("wgA"))

    for tt in range(NTT):
        if tt == 1:
            p.emit("vector", lambda e: e.nop(),
                   waits=[("xps", p.val("xps"))])
        ln_done = emit_ln(p, io, sb, io["x_mid"], tt)
        for dc in range(DC):
            p.emit("gpsimd",
                   lambda e, d=dc, t_=tt: e.dma_start(
                       out=io["xh_part"][t_, :, d, :],
                       in_=sb["xh"][:, d * TT:(d + 1) * TT]),
                   waits=[("dve", ln_done)], inc="xps", by=16)
        p.emit("gpsimd",
               lambda e, t_=tt: e.collective_compute(
                   "AllGather", A.bypass, replica_groups=[list(range(NC))],
                   ins=[io["xh_part"][t_].rearrange("p c t -> p (c t)")],
                   outs=[io["xh_all"][t_ * NC:(t_ + 1) * NC]
                         .rearrange("a p c t -> (a p) (c t)")]),
               waits=[("xps", p.val("xps"))], inc="ccx")
    xps_all = p.val("xps")

    def slabv(oc, d):
        if oc < 4:
            return sb["x2"][:, oc * D + d * 128:oc * D + (d + 1) * 128]
        return wsb[:, (oc - 4) * D + d * 128:(oc - 4) * D + (d + 1) * 128]

    wld = {}
    ev = {}
    for tg, (ttl, rank) in enumerate(
            (t, r) for t in range(NTT) for r in range(NC)):
        gt = rank * NTT + ttl
        w = [("pe", ev.get(("pe_last", tg - 1), 0)), ("ccx", ttl + 1)]
        if tg < 2:
            w.append(("xps", xps_all))
        p.emit("sync",
               lambda e, t_=ttl, r=rank: e.dma_start(
                   out=sb["xh"][:, :],
                   in_=io["xh_all"][t_ * NC + r].rearrange("p c t -> p (c t)")),
               waits=w, inc="xhl", by=16)
        xh_ld = ("xhl", p.val("xhl"))
        rbuf = tg % 2
        rpsem = "rpA" if rbuf == 0 else "rpB"
        p.emit("sync",
               lambda e, b=rbuf, t_=gt: e.dma_start(
                   out=sb["rc"][:, b * TT:(b + 1) * TT],
                   in_=io["ropec"][:, t_ * TT:(t_ + 1) * TT]),
               waits=[("dve", ev.get(("rope_dve", tg - 2), 0))], inc=rpsem, by=16)
        p.emit("sync",
               lambda e, b=rbuf, t_=gt: e.dma_start(
                   out=sb["rs"][:, b * TT:(b + 1) * TT],
                   in_=io["ropes"][:, t_ * TT:(t_ + 1) * TT]),
               inc=rpsem, by=16)
        rope_ld = (rpsem, p.val(rpsem))
        for oc in range(6):
            gi = tg * 6 + oc
            bank = sb[f"ps{oc % 2}"]
            for dc in range(DC):
                w = []
                if dc == 0:
                    w = [slab_ld, xh_ld,
                         ("dve", ev.get(("cons", gi - 2), 0)),
                         ("act", ev.get(("vcons", gi - 2), 0))]
                p.emit("tensor",
                       lambda e, o=oc, d=dc, b=bank: e.matmul(
                           b[:, :], slabv(o, d),
                           sb["xh"][:, d * TT:(d + 1) * TT],
                           start=(d == 0), stop=(d == DC - 1)),
                       waits=w, inc="pe")
            wld[gi] = p.val("pe")
            mmdone = p.val("pe")
            if oc < 5:
                rcv = sb["rc"][:, rbuf * TT:(rbuf + 1) * TT]
                rsv = sb["rs"][:, rbuf * TT:(rbuf + 1) * TT]
                p.emit("vector",
                       lambda e, b=bank, r=rcv: e.tensor_tensor(
                           sb["va"][:, :], b[:, :], r, A.mult),
                       waits=[("pe", mmdone), rope_ld], inc="dve")
                p.emit("vector",
                       lambda e, b=bank, r=rsv: e.tensor_tensor(
                           sb["vb"][0:64, :], b[64:128, :], r[0:64, :], A.mult),
                       inc="dve")
                p.emit("vector",
                       lambda e, b=bank, r=rsv: e.tensor_tensor(
                           sb["vb"][64:128, :], b[0:64, :], r[64:128, :], A.mult),
                       inc="dve")
                ev[("cons", gi)] = p.val("dve")
                if oc < 4:
                    qb2 = (tg * 4 + oc) % 2
                    dstv = sb["qd"][:, qb2 * TT:(qb2 + 1) * TT]
                    qssem = "qsA" if qb2 == 0 else "qsB"
                    p.emit("vector",
                           lambda e, o=dstv: e.tensor_tensor(
                               o, sb["va"][:, :], sb["vb"][:, :], A.add),
                           waits=[ev.get(("qst", tg * 4 + oc - 2), (qssem, 0))],
                           inc="dve")
                    qdve = p.val("dve")
                    p.emit("gpsimd",
                           lambda e, o=dstv, h=oc, t_=gt: e.dma_start(
                               out=io["q_dram"][h, :, t_ * TT:(t_ + 1) * TT], in_=o),
                           waits=[("dve", qdve)], inc=qssem, by=16)
                    ev[("qst", tg * 4 + oc)] = (qssem, p.val(qssem))
                else:
                    p.emit("vector",
                           lambda e, t_=gt: e.tensor_tensor(
                               sb["karena"][:, t_ * TT:(t_ + 1) * TT],
                               sb["va"][:, :], sb["vb"][:, :], A.add),
                           inc="dve")
                ev[("rope_dve", tg)] = p.val("dve")
                ev[("cons", gi)] = p.val("dve")
            else:
                p.emit("vector",
                       lambda e, b=bank: e.tensor_copy(sb["va"][:, :], b[:, :]),
                       waits=[("pe", mmdone),
                              ("pe", ev.get(("vtr", tg - 1), 0))], inc="dve")
                ev[("cons", gi)] = p.val("dve")
                vdve = p.val("dve")
                for j in range(4):
                    b2 = sb["ps6"] if j < 2 else sb["ps7"]
                    col = (j % 2) * 128
                    p.emit("tensor",
                           lambda e, bb=b2, c=col, j_=j: e.matmul(
                               bb[:, c:c + 128],
                               sb["va"][:, j_ * 128:(j_ + 1) * 128],
                               sb["idn"][:, :], start=(j_ % 2 == 0),
                               stop=(j_ % 2 == 1), is_transpose=True),
                           waits=[("dve", vdve),
                                  ("act", ev.get(("vcons", gi - 1), 0))]
                           if j == 0 else (),
                           inc="pe")
                ev[("vtr", tg)] = p.val("pe")
                p.emit("scalar",
                       lambda e, t_=gt: e.copy(
                           sb["varena"][:, (t_ * 4) * 128:(t_ * 4 + 2) * 128],
                           sb["ps6"][:, 0:256]),
                       waits=[("pe", ev[("vtr", tg)])], inc="act")
                p.emit("scalar",
                       lambda e, t_=gt: e.copy(
                           sb["varena"][:, (t_ * 4 + 2) * 128:(t_ * 4 + 4) * 128],
                           sb["ps7"][:, 0:256]),
                       inc="act")
                ev[("vcons", gi)] = p.val("act")
        ev[("pe_last", tg)] = p.val("pe")



def phase_attn(p, io, sb):
    eb, em = sb["eb"], sb["em"]
    o_bank, l_bank = sb["ps4"], sb["ps5"]
    qld = {}
    prev_fin_dve = 0
    prev_lb_dve = 0
    prev_pair_exp = [0, 0]
    for qh in range(4):
        for qb in range(16):
            gi = qh * 16 + qb
            qbuf = gi % 2
            qdsem = "qdA" if qbuf == 0 else "qdB"
            p.emit("sync",
                   lambda e, h=qh, b=qb, q_=qbuf: e.dma_start(
                       out=sb["qd"][:, q_ * TT:(q_ + 1) * TT],
                       in_=io["q_dram"][h, :, b * TT:(b + 1) * TT]),
                   waits=[("pe", qld.get(("pe", gi - 2), 0))], inc=qdsem, by=16)
            qld[gi] = (qdsem, p.val(qdsem))
            nhb = 2 * (qb + 1)
            exp_m = {}
            lpv_m = {}
            cons = {}

            def lpv(hb, last):
                c_, cw = cons[hb]
                first = (hb == 0)
                if first:
                    cw = cw + [("dve", prev_fin_dve)]
                for kc in range(2):
                    p.emit("tensor",
                           lambda e, c=c_, k=kc, f=first: e.matmul(
                               l_bank[0:1, 0:TT], sb["ones_c"][:, :],
                               c[:, k * TT:(k + 1) * TT],
                               start=(f and k == 0), stop=(last and k == 1)),
                           waits=cw if kc == 0 else (), inc="pe")
                for kc in range(2):
                    gc = 2 * hb + kc
                    p.emit("tensor",
                           lambda e, c=c_, k=kc, g=gc, f=first: e.matmul(
                               o_bank[:, :],
                               sb["varena"][:, g * 128:(g + 1) * 128],
                               c[:, k * TT:(k + 1) * TT],
                               start=(f and k == 0), stop=(last and k == 1)),
                           inc="pe")
                lpv_m[hb] = p.val("pe")

            for hb in range(nhb):
                pair = sb["psA"].ap()[:, (hb % 2) * 1024:(hb % 2 + 1) * 1024]
                for kc in range(2):
                    w = []
                    if kc == 0:
                        w = [("act", exp_m.get(hb - 2, prev_pair_exp[hb % 2]))]
                        if hb == 0:
                            w.append(qld[gi])
                    p.emit("tensor",
                           lambda e, pr=pair, k=(2 * hb + kc) * 128, c=kc,
                           q_=qbuf: e.matmul(
                               pr[:, c * 512:(c + 1) * 512],
                               sb["karena"][:, k:k + 128],
                               sb["qd"][:, q_ * TT:(q_ + 1) * TT],
                               start=True, stop=True),
                           waits=w, inc="pe")
                s_done = p.val("pe")
                if hb > 0:
                    lpv(hb - 1, False)
                ebv = eb[:, (hb % 2) * 1024:(hb % 2 + 1) * 1024]
                p.emit("scalar",
                       lambda e, o=ebv, pr=pair: e.activation(
                           out=o, in_=pr, func=AF.Exp, scale=SCALE),
                       waits=[("pe", s_done), ("pe", lpv_m.get(hb - 2, 0))],
                       inc="act")
                exp_m[hb] = p.val("act")
                if hb // 2 == qb:
                    emv = em[:, (hb % 2) * 1024:(hb % 2 + 1) * 1024]
                    p.emit("vector",
                           lambda e, o=emv, s_=ebv, m=sb["mk"][
                               :, (hb % 2) * 1024:(hb % 2 + 1) * 1024]:
                           e.tensor_tensor(o, s_, m, A.mult),
                           waits=[("act", exp_m[hb]),
                                  ("pe", lpv_m.get(hb - 1, 0))], inc="dve")
                    cons[hb] = (emv, [("dve", p.val("dve"))])
                else:
                    cons[hb] = (ebv, [("act", exp_m[hb])])
            lpv(nhb - 1, True)
            prev_pair_exp = [exp_m[nhb - 2], exp_m[nhb - 1]]
            pv_done = p.val("pe")
            r_row = sb["rows"][0:1, 3 * TT:4 * TT]
            p.emit("vector", lambda e, r=r_row: e.reciprocal(r, l_bank[0:1, 0:TT]),
                   waits=[("pe", pv_done)], inc="dve")
            p.emit("scalar",
                   lambda e, r=r_row: e.activation(
                       out=sb["r16"][0:1, :], in_=r, func=AF.Copy,
                       bias=0.0, scale=1.0),
                   waits=[("dve", p.val("dve"))], inc="act")
            p.emit("scalar",
                   lambda e, r=r_row: e.copy(sb["rows"][0:1, 0:TT], r),
                   inc="act")
            p.emit("tensor",
                   lambda e: e.matmul(sb["ps6"][:, :], sb["ones_r16"][:, :],
                                      sb["r16"][0:1, :], start=True, stop=True),
                   waits=[("act", p.val("act")), ("dve", prev_lb_dve)], inc="pe")
            p.emit("vector",
                   lambda e: e.tensor_copy(sb["lb"][:, :], sb["ps6"][:, :]),
                   waits=[("pe", p.val("pe"))], inc="dve")
            prev_lb_dve = p.val("dve")
            p.emit("vector",
                   lambda e, b=qb: e.tensor_tensor(
                       sb["cstg"][:, (b % 2) * TT:(b % 2 + 1) * TT],
                       o_bank[:, :], sb["lb"][:, :], A.mult),
                   waits=[("ctp", qld.get(("st", gi - 1), 0))], inc="dve")
            prev_fin_dve = p.val("dve")
            qld[("pe", gi)] = p.val("pe")
            p.emit("gpsimd",
                   lambda e, h=qh, b=qb: e.dma_start(
                       out=io["ctx_part"][b // 2, h, :, (b % 2) * TT:(b % 2 + 1) * TT],
                       in_=sb["cstg"][:, (b % 2) * TT:(b % 2 + 1) * TT]),
                   waits=[("dve", prev_fin_dve)], inc="ctp", by=16)
            qld[("st", gi)] = p.val("ctp")

    for oc in range(2):
        p.emit("gpsimd",
               lambda e, b=oc, o=oc: e.dma_start(
                   out=sb["wsb"][:, b * D:(b + 1) * D],
                   in_=io["wo_s"][o].rearrange("p c l -> p (c l)")),
               waits=[("act", p.val("act"))],
               inc="wgA" if oc == 0 else "wgB", by=16)

    def do_a2a(e):
        with allow_ici_all_to_all():
            return e.collective_compute(
                "AllToAll", A.bypass, replica_groups=[list(range(NC))],
                ins=[io["ctx_part"].rearrange("a h p t -> (a h p) t")],
                outs=[io["ctx_own"].rearrange("a h p t -> (a h p) t")])

    p.emit("gpsimd", do_a2a, waits=[("ctp", p.val("ctp"))], inc="ccc")
    phase_wo(p, io, sb)



def phase_wo(p, io, sb):
    wsb = sb["wsb"]
    attn_act = p.val("act")
    attn_dve = p.val("dve")

    def ctxv(c):
        if c < 16:
            return sb["xh"][:, c * 1024:(c + 1) * 1024]
        return sb["x2"][:, (c - 16) * 1024:(c - 15) * 1024]

    for c in range(DC):
        for t2 in range(2):
            p.emit("sync",
                   lambda e, c_=c, t_=t2: e.dma_start(
                       out=ctxv(c_)[:, t_ * TT:(t_ + 1) * TT],
                       in_=io["ctx_own"][c_ // 4, c_ % 4, :,
                                         t_ * TT:(t_ + 1) * TT]),
                   waits=[("ccc", 1)], inc="cxl", by=16)
    ctx_ld = ("cxl", p.val("cxl"))
    wld = {}
    ev = {}
    st = {}
    for oc in range(DC):
        buf = oc % 2
        wgsem = "wgA" if buf == 0 else "wgB"
        if oc >= 2:
            p.emit("gpsimd",
                   lambda e, b=buf, o=oc: e.dma_start(
                       out=wsb[:, b * D:(b + 1) * D],
                       in_=io["wo_s"][o].rearrange("p c l -> p (c l)")),
                   waits=[("pe", wld.get(oc - 2, 0))], inc=wgsem, by=16)
        wl = (wgsem, p.val(wgsem))
        xsem = "xsA" if buf == 0 else "xsB"
        xbuf = sb["xs"][:, 0:2 * TT] if buf == 0 else sb["rs"][:, 0:2 * TT]
        p.emit("sync",
               lambda e, o=oc, xb=xbuf: e.dma_start(
                   out=xb,
                   in_=io["x_mid"][o * 128:(o + 1) * 128, :]),
               waits=[("dve", ev.get(oc - 2, attn_dve))], inc=xsem, by=16)
        xl = (xsem, p.val(xsem))
        for t2 in range(2):
            bank = sb[f"ps{buf * 2 + t2}"]
            for dc in range(DC):
                w = []
                if dc == 0 and t2 == 0:
                    w = [wl, ctx_ld,
                         ("dve", ev.get(oc - 2, attn_dve)),
                         ("act", attn_act)]
                p.emit("tensor",
                       lambda e, b=bank, bf=buf, d=dc, t_=t2: e.matmul(
                           b[:, :],
                           wsb[:, bf * D + d * 128:bf * D + (d + 1) * 128],
                           ctxv(d)[:, t_ * TT:(t_ + 1) * TT],
                           start=(d == 0), stop=(d == DC - 1)),
                       waits=w, inc="pe")
        wld[oc] = p.val("pe")
        obuf = sb["ostg"][:, 0:2 * TT] if buf == 0 else sb["rc"][:, 0:2 * TT]
        w = [("pe", wld[oc]), xl, st.get(oc - 2, ("stA", 0))]
        for t2 in range(2):
            bank = sb[f"ps{buf * 2 + t2}"]
            p.emit("vector",
                   lambda e, b=bank, t_=t2, ob=obuf, xb=xbuf:
                   e.scalar_tensor_tensor(
                       ob[:, t_ * TT:(t_ + 1) * TT], b[:, :], 0.0,
                       xb[:, t_ * TT:(t_ + 1) * TT], A.add, A.add),
                   waits=w if t2 == 0 else (), inc="dve")
        ev[oc] = p.val("dve")
        ssem = "stA" if buf == 0 else "stB"
        p.emit("gpsimd",
               lambda e, o=oc, ob=obuf: e.dma_start(
                   out=io["x_attn"][o * 128:(o + 1) * 128, :],
                   in_=ob),
               waits=[("dve", ev[oc])], inc=ssem, by=16)
        st[oc] = (ssem, p.val(ssem))



def _rope_tables():
    import jax
    import jax.numpy as jnp
    cpu = jax.local_devices(backend="cpu")[0]
    with jax.default_device(cpu):
        return _rope_tables_impl(jnp)


def _rope_tables_impl(jnp):
    ROPE = dict(rope_theta=500000.0, factor=32.0, hi_freq_factor=4.0,
                lo_freq_factor=1.0, original_context_length=8192)
    head_dim = DH
    pos = jnp.arange(T)
    idx = jnp.arange(0, head_dim, 2, dtype=jnp.float32) / head_dim
    freq = 1.0 / (2.0 * math.pi) * ROPE['rope_theta'] ** (-idx)
    factor, lo, hi = (ROPE['factor'], ROPE['lo_freq_factor'], ROPE['hi_freq_factor'])
    L0 = ROPE['original_context_length']
    freq_low, freq_high = (lo / L0, hi / L0)
    freq_scaled = jnp.where(freq < freq_low, freq / factor, freq)
    smooth = jnp.clip((L0 * freq - lo) / (hi - lo), 0.0, 1.0)
    freq_smooth = (1.0 - smooth) * (freq / factor) + smooth * freq
    is_mid = (freq >= freq_low) & (freq <= freq_high)
    freq = jnp.where(is_mid, freq_smooth, freq_scaled)
    phase = 2.0 * math.pi * pos.astype(jnp.float32)[:, None] * freq[None, :]
    emb = jnp.concatenate([phase, phase], axis=-1)
    cos = np.asarray(jnp.cos(emb), dtype=np.float32)
    sin = np.asarray(jnp.sin(emb), dtype=np.float32)
    cos_d = np.ascontiguousarray(cos.T)
    sinrot = np.ascontiguousarray(sin.T)
    sinrot[0:64, :] *= -1.0
    return cos_d, sinrot


def _slab(w):
    Dd, M = w.shape
    return np.ascontiguousarray(
        w.reshape(Dd // 128, 128, M // 128, 128).transpose(2, 1, 0, 3))


def kernel(L0_attn_resid, L0_attn_Wq, L0_attn_Wk, L0_attn_Wv, L0_attn_Wo,
           L0_ffn_w_up_codes, L0_ffn_w_up_absmax, L0_ffn_b_up,
           L0_ffn_w_up_lora_a, L0_ffn_w_up_lora_b, L0_ffn_w_down_codes,
           L0_ffn_w_down_absmax, L0_ffn_b_down, L0_ffn_w_down_lora_a,
           L0_ffn_w_down_lora_b):
    resid = np.ascontiguousarray(np.asarray(L0_attn_resid, dtype=np.float32)[0].T)
    wq = np.asarray(L0_attn_Wq, dtype=np.float32)
    wk = np.asarray(L0_attn_Wk, dtype=np.float32)
    wv = np.asarray(L0_attn_Wv, dtype=np.float32)
    wo = np.asarray(L0_attn_Wo, dtype=np.float32)
    upc = np.asarray(L0_ffn_w_up_codes, dtype=np.int32).reshape(H, D)
    upam = np.asarray(L0_ffn_w_up_absmax, dtype=np.float32).reshape(H, D // BLK)
    dnc = np.asarray(L0_ffn_w_down_codes, dtype=np.int32).reshape(D, H)
    dnam = np.asarray(L0_ffn_w_down_absmax, dtype=np.float32).reshape(D, H // BLK)
    bup = np.asarray(L0_ffn_b_up, dtype=np.float32)
    bdn = np.asarray(L0_ffn_b_down, dtype=np.float32)
    aup = np.asarray(L0_ffn_w_up_lora_a, dtype=np.float32)
    bu = np.asarray(L0_ffn_w_up_lora_b, dtype=np.float32)
    adn = np.asarray(L0_ffn_w_down_lora_a, dtype=np.float32)
    bd = np.asarray(L0_ffn_w_down_lora_b, dtype=np.float32)

    cos_d, sinrot = _rope_tables()
    masks = np.zeros((128, 4 * TT), dtype=ml_dtypes.bfloat16)
    for kc in range(4):
        kg = kc * 128 + np.arange(128)[:, None]
        qg = np.arange(TT)[None, :]
        masks[:, kc * TT:(kc + 1) * TT] = (kg <= qg).astype(ml_dtypes.bfloat16)
    ident = np.eye(128, dtype=np.float32)

    wo_slab = _slab(wo)
    bup_h = np.ascontiguousarray(bup.reshape(HC, 128).T)
    aupT = np.ascontiguousarray(aup.T)
    adnT = np.ascontiguousarray(adn.T)

    in_maps = []
    for c in range(NC):
        t0 = c * TOWN
        slabs = list(range(c, HC, NC))
        upc_sl = np.zeros((UPS * 128, D), dtype=np.int32)
        upam_sl = np.zeros((UPS, 128, BLK), dtype=np.float32)
        bu_sl = np.zeros((16, UPS * 128), dtype=np.float32)
        for j, g in enumerate(slabs):
            upc_sl[j * 128:(j + 1) * 128] = upc[g * 128:(g + 1) * 128]
            upam_sl[j] = upam[g * 128:(g + 1) * 128]
            bu_sl[:, j * 128:(j + 1) * 128] = bu[:, g * 128:(g + 1) * 128]
        upam_sl = np.ascontiguousarray(
            upam_sl.transpose(1, 0, 2).reshape(128, UPS * BLK))
        d0 = c * 512
        dnc_sl = np.ascontiguousarray(dnc[d0:d0 + 512])
        dnam_sl = np.ascontiguousarray(
            dnam[d0:d0 + 512].reshape(4, 128, H // BLK).transpose(1, 0, 2)
            .reshape(128, 4 * (H // BLK)))
        bd_sl = np.ascontiguousarray(bd[:, d0:d0 + 512])

        in_maps.append({
            "resid_in": np.ascontiguousarray(resid[:, t0:t0 + TOWN]),
            "wq_s": _slab(wq[:, c * 512:(c + 1) * 512]),
            "wk_s": _slab(wk[:, c * 128:(c + 1) * 128]),
            "wv_s": _slab(wv[:, c * 128:(c + 1) * 128]),
            "wo_s": wo_slab,
            "upc": upc_sl,
            "upam": upam_sl,
            "dnc": dnc_sl,
            "dnam": dnam_sl,
            "bup": bup_h,
            "bdn": np.ascontiguousarray(bdn.reshape(DC, 128).T),
            "aupT": aupT,
            "adnT": adnT,
            "bu_sl": bu_sl,
            "bd_sl": bd_sl,
            "ropec": cos_d,
            "ropes": sinrot,
            "masks": masks,
            "ident": ident,
        })

    nc = build_program()
    res = run_bass_kernel_spmd(nc, in_maps, core_ids=list(range(NC)))
    out = np.empty((1, T, D), dtype=np.float32)
    for c in range(NC):
        out[0, c * TOWN:(c + 1) * TOWN, :] = res.results[c]["out_d"].T
    return out
